# version 50
# baseline (speedup 1.0000x reference)
import numpy as np
import ml_dtypes
import concourse.bass as bass
import concourse.mybir as mybir
from concourse.bass_utils import run_bass_kernel_spmd

F32 = mybir.dt.float32
BF16 = mybir.dt.bfloat16
AF = mybir.ActivationFunctionType
OP = mybir.AluOpType
AX = mybir.AxisListType
NPBF = ml_dtypes.bfloat16

S, D, H, HD, FH = 2048, 1024, 16, 64, 4096
NQT, NKB = 16, 16
EPS = 1e-5
SLOPES = [2.0 ** (-0.5 * (h + 1)) for h in range(H)]
DMAX = [min(15, int((65.0 / SLOPES[h] + 127.0) // 128)) for h in range(H)]

def head_sched(h):
    dm = DMAX[h]
    spans = []
    chunks = []
    for kb in range(NKB):
        q0 = kb * 128
        qend = min(S, 128 * (kb + dm + 1))
        spans.append((q0, qend))
        ch = []
        c0 = q0
        while c0 < qend:
            c1 = min(qend, (c0 // 512 + 1) * 512)
            ch.append((c0, c1))
            c0 = c1
        chunks.append(ch)
    kmin = [max(0, 4 * qg - dm) for qg in range(4)]
    kmax = [min(15, 4 * qg + 3) for qg in range(4)]
    return spans, chunks, kmin, kmax

SCHED = [head_sched(h) for h in range(H)]
NCHUNK = [sum(len(c) for c in SCHED[h][1]) for h in range(H)]
CHUNKS_BEFORE = [0] * (H + 1)
for h in range(H):
    CHUNKS_BEFORE[h + 1] = CHUNKS_BEFORE[h] + NCHUNK[h]


class Sem:
    def __init__(self, nc, name):
        self.h = nc.alloc_semaphore(name=name)
        self.n = 0
    def bump(self, k=1):
        self.n += k
        return self.n


class Plan:
    ENGS = ("tensor", "vector", "scalar", "sync", "gpsimd")
    def __init__(self):
        self.ops = {e: [] for e in self.ENGS}
    def op(self, eng, fn, waits=(), incs=()):
        ws = [(s.h, int(v)) for (s, v) in waits if v > 0]
        ks = []
        for (s, k) in incs:
            s.bump(k)
            ks.append((s.h, k))
        self.ops[eng].append((fn, ws, ks))
    def emit(self, nc):
        with nc.allow_low_precision(reason="bf16 matmul operands by design; fp32 accumulation in psum"), nc.Block() as block:
            for eng in self.ENGS:
                ops = self.ops[eng]
                if not ops:
                    continue
                def run(engine, _ops=ops):
                    for fn, ws, ks in _ops:
                        for h, v in ws:
                            engine.wait_ge(h, v)
                        inst = fn(engine)
                        if ks:
                            assert inst is not None
                            inst.then_inc(*ks[0])
                            for h, k in ks[1:]:
                                engine.nop().then_inc(h, k)
                getattr(block, eng)(run)


def build_nc(dbg=False, nheads=H, dumps=()):
    nc = bass.Bass("TRN2", target_bir_lowering=False, detect_race_conditions=False)

    d_x   = nc.dram_tensor("x",    [S, D],   F32, kind="ExternalInput").ap()
    d_wq  = nc.dram_tensor("wq",   [D, D],   F32, kind="ExternalInput").ap()
    d_wk  = nc.dram_tensor("wk",   [D, D],   F32, kind="ExternalInput").ap()
    d_wv  = nc.dram_tensor("wv",   [D, D],   F32, kind="ExternalInput").ap()
    d_wo  = nc.dram_tensor("wo",   [D, D],   F32, kind="ExternalInput").ap()
    d_w1t = nc.dram_tensor("w1t",  [D, FH],  F32, kind="ExternalInput").ap()
    d_w2t = nc.dram_tensor("w2t",  [FH, D],  F32, kind="ExternalInput").ap()
    d_bqB = nc.dram_tensor("bqB",  [128, 8], F32, kind="ExternalInput").ap()
    d_bkB = nc.dram_tensor("bkB",  [128, 8], F32, kind="ExternalInput").ap()
    d_b1B = nc.dram_tensor("b1B",  [128, 32], F32, kind="ExternalInput").ap()
    d_bv  = nc.dram_tensor("bv",   [1, D],   F32, kind="ExternalInput").ap()
    d_bo  = nc.dram_tensor("bo",   [1, D],   F32, kind="ExternalInput").ap()
    d_b2  = nc.dram_tensor("b2",   [1, D],   F32, kind="ExternalInput").ap()
    d_scl = nc.dram_tensor("scl",  [1, 8],   F32, kind="ExternalInput").ap()
    d_ali = nc.dram_tensor("alibi",[128, 256], F32, kind="ExternalInput").ap()
    d_qps = nc.dram_tensor("qpos", [128, 16], F32, kind="ExternalInput").ap()
    d_idb = nc.dram_tensor("identb", [128, 128], BF16, kind="ExternalInput").ap()
    d_mmk = nc.dram_tensor("minmask", [128, 128], BF16, kind="ExternalInput").ap()
    d_amk = nc.dram_tensor("amask", [128, 128], F32, kind="ExternalInput").ap()
    d_one = nc.dram_tensor("onescol", [1, 128], BF16, kind="ExternalInput").ap()
    d_sel = nc.dram_tensor("selmat", [16, 2048], BF16, kind="ExternalInput").ap()
    d_out = nc.dram_tensor("out",  [S, D],   F32, kind="ExternalOutput").ap()
    kindd = "ExternalOutput" if dbg else "Internal"
    d_xnd = nc.dram_tensor("xn_scratch", [128, 16384], BF16, kind=kindd).ap()
    d_std = nc.dram_tensor("stage_scratch", [128, 16384], BF16, kind=kindd).ap()

    base = None
    arena = nc.alloc_sbuf_tensor("arena", [128, 210688], mybir.dt.uint8)
    base = nc.lookup_mloc(arena).addr

    def at(name, off, shape, dt):
        return nc.alloc_sbuf_tensor_at(name, shape, dt, offset=base + off).ap()

    xn_FM  = at("xn_FM",  0,     [128, 16384], BF16)
    qaug   = [at(f"qaug{i}", 0 + 4096 * i, [65, 2048], BF16) for i in range(2)]
    kaug   = [at(f"kaug{i}", 8192 + 4096 * i, [65, 2048], BF16) for i in range(2)]
    P_T    = ([at(f"pt{i}", 16384 + 4096 * i, [128, 2048], BF16) for i in range(2)]
              + [at(f"pt{i}", 177920 + 4096 * (i - 2), [128, 2048], BF16) for i in range(2, 4)]
              + [at("pt4", 189696, [128, 2048], BF16)])
    accS   = [at(f"accs{i}", 24576 + 1024 * i, [64, 512], BF16) for i in range(2)]
    vT_sb  = at("vt_sb", 188672, [4, 512], BF16)
    m_all  = at("m_all", 78784, [128, 16], F32)
    vbtmp  = at("vbtmp", 78848, [128, 16], F32)
    vb     = at("vb",    78912, [128, 16], BF16)
    hgel   = at("hgel",  0,     [128, 8192], BF16)
    xnring = [at(f"xnr{i}", 16384 + 4096 * i, [128, 2048], BF16) for i in range(2)]
    sgring = [at(f"sgr{i}", 24576 + 2048 * i, [128, 1024], BF16) for i in range(2)]
    xring  = [at(f"xr{i}", 32768 + 8192 * i, [128, 2048], F32) for i in range(3)]
    sqt    = [at(f"sq{i}", 57344 + 2048 * i, [128, 1024], BF16) for i in range(2)]
    xntm   = [at(f"xtm{i}", 61440 + 2048 * i, [128, 1024], BF16) for i in range(2)]
    attnFM = at("attnFM", 32768, [128, 16384], BF16)
    xres   = [at(f"xres{i}", 32768 + 4096 * i, [128, 1024], F32) for i in range(2)]
    outsb  = [at(f"osb{i}", 40960 + 4096 * i, [128, 1024], F32) for i in range(2)]

    alibi  = at("alibi_sb", 65536, [128, 256], F32)
    qpos   = at("qpos_sb", 66560, [128, 16], F32)
    identB = at("identb_sb", 66624, [128, 128], BF16)
    minmask= at("minmask_sb", 66880, [128, 128], BF16)
    onescol= at("onescol_sb", 67136, [1, 128], BF16)
    lora   = at("loratoks", 67392, [16, 1024], BF16)
    bqB    = at("bqB_sb", 69440, [128, 8], F32)
    bkB    = at("bkB_sb", 69472, [128, 8], F32)
    b1B    = at("b1B_sb", 69504, [128, 32], F32)
    sclsb  = at("scl_sb", 69632, [1, 8], F32)
    bvbf   = at("bvbf", 69664, [1, 1024], BF16)
    bb2    = at("bb2", 71712, [2, 1024], BF16)
    ones2  = at("ones2", 73760, [2, 128], BF16)
    oddcT  = at("oddcT", 75808, [128, 128], F32)
    lnst   = at("lnst", 76320, [128, 96], F32)
    scalebf = at("scalebf", 76672, [1, 8], BF16)
    scale_rep = at("scale_rep", 76704, [128, 8], F32)
    evr1 = [at(f"evr1_{i}", 76736 + 1024 * i, [128, 512], BF16) for i in range(2)]
    evr2 = [at(f"evr2_{i}", 76736 + 512 * i, [128, 256], BF16) for i in range(2)]
    amask  = at("amask_sb", 186112, [128, 128], F32)
    lnrow  = at("lnrow", 186624, [1, 512], F32)
    recB   = at("recB", 27904, [1, 2048], BF16)
    zcol   = at("zcol", 32000, [128, 72], BF16)
    selmat = at("selmat", 28672, [16, 2048], BF16)
    Q_FM   = at("Q_FM", 79104, [128, 16384], BF16)
    K_FM   = at("K_FM", 111872, [128, 16384], BF16)
    w1t_sb = at("w1t_sb", 79104, [128, 32768], BF16)
    V_pad  = at("V_pad", 144640, [128, 16640], BF16)
    wbuf   = [at(f"wbuf{i}", 177920 + 16384 * i, [128, 8192], BF16) for i in range(2)]
    w2t_sb = at("w2t_sb", 144640, [128, 32768], BF16)
    dstage = at("dstage", 210688, [128, 1024], BF16)

    sem_names = ["sx0", "sx0h", "sx1", "sx2", "smisc", "swq", "swk", "swv", "swo", "sw1", "sw2", "slt",
                 "saugq0", "saugq1", "saugk0", "saugk1", "saugv0", "saugv1",
                 "sxnr0", "sxnr1", "ssgr0", "ssgr1", "sxre0", "sxre1", "sout0", "sout1", "sw2b", "samk", "sdst0", "sdst1",
                 "sxnsp", "sxnsp2", "sxnsp3", "sxnsp4", "saug", "sxnr", "ssgr", "sdst", "sxre", "sout",
                 "aln_v", "aln_s", "aln_sq", "asum", "axn", "atp", "axnE",
                 "bmm", "bev",
                 "cdg", "cdgE", "cvb", "cvt", "cvtE", "csc", "cexp", "cmsk",
                 "cavk", "cavq", "crec", "crep", "cacc", "catt", "cmax",
                 "dmm", "dev",
                 "d2oc", "d2ocE", "d2sc", "d2er", "d2erE", "d2f", "d2fw2", "d2ev", "d2evw2",
                 "ep1", "egel", "ep2", "epf", "esd", "ssel", "sbias", "cpm", "vch"]
    SM = {n: Sem(nc, n) for n in sem_names}
    g = lambda n: SM[n]

    A = Plan()
    A.op("sync", lambda e: e.dma_start(
        xring[0][:, 0:1024], d_x[0:128, :]), incs=[(g("sx0h"), 16)])
    A.op("sync", lambda e: e.dma_start(
        xring[0][:, 1024:2048], d_x[128:256, :]), incs=[(g("sx0"), 16)])
    A.op("sync", lambda e: e.dma_start(
        xring[1][:, :].rearrange("p (j d) -> p j d", j=2),
        d_x[256:512, :].rearrange("(j p) d -> p j d", p=128)),
        incs=[(g("sx1"), 16)])
    for (dst, srcap) in [(alibi, d_ali), (qpos, d_qps), (identB, d_idb), (minmask, d_mmk),
                         (onescol, d_one), (bqB, d_bqB), (bkB, d_bkB), (b1B, d_b1B),
                         (sclsb, d_scl)]:
        A.op("sync", lambda e, a=dst, b=srcap: e.dma_start(a[:, :], b[:, :]),
             incs=[(g("smisc"), 16)])
    for (dst, srcap) in [(bvbf, d_bv), (bb2[0:1, :], d_bo), (bb2[1:2, :], d_b2)]:
        A.op("gpsimd", lambda e, a=dst, b=srcap: e.dma_start(a[:, :], b[:, :]),
             incs=[(g("sbias"), 16)])
    A.op("gpsimd", lambda e: e.dma_start(
        wbuf[0][:, :].rearrange("p (c o) -> p c o", c=8),
        d_wq.rearrange("(c p) o -> p c o", p=128)), incs=[(g("swq"), 16)])
    A.op("gpsimd", lambda e: e.dma_start(
        wbuf[1][:, :].rearrange("p (c o) -> p c o", c=8),
        d_wk.rearrange("(c p) o -> p c o", p=128)), incs=[(g("swk"), 16)])
    vones = V_pad[:, :].rearrange("p (t h e) -> p t h e", t=16, h=16)[:, :, :, 64:65]
    A.op("vector", lambda e: e.memset(vones, 1.0))
    A.op("vector", lambda e: e.memset(ones2[:, :], 1.0))
    for pr in range(2, 8):
        waits = [(g("axn"), 2 * pr - 4)] if pr >= 3 else []
        A.op("sync", lambda e, pr=pr: e.dma_start(
            xring[pr % 3][:, :].rearrange("p (j d) -> p j d", j=2),
            d_x[pr * 256:(pr + 1) * 256, :].rearrange("(j p) d -> p j d", p=128)),
            waits=waits, incs=[(g("sx%d" % (pr % 3)), 16)])
        if pr == 3:
            A.op("sync", lambda e: e.dma_start(lora[:, :], xntm[0][1:17, :]),
                 waits=[(g("axn"), 1)], incs=[(g("slt"), 16)])
    use = 0

    def xwait(qt):
        if qt == 0:
            return (g("sx0h"), 16)
        return (g("sx%d" % ((qt // 2) % 3)), 16 * ((qt // 2) // 3 + 1))

    def emit_square(q2):
        xr2 = xring[(q2 // 2) % 3][:, (q2 % 2) * 1024:(q2 % 2) * 1024 + 1024]
        waits = [xwait(q2)]
        if q2 >= 2:
            waits.append((g("asum"), q2 - 1))
        A.op("scalar", lambda e, xr2=xr2, q2=q2: e.activation(sqt[q2 % 2][:, :], xr2, AF.Square),
             waits=waits, incs=[(g("aln_sq"), 1)])

    def emit_tile_sums(qt):
        emit_square(qt)
        xr = xring[(qt // 2) % 3][:, (qt % 2) * 1024:(qt % 2) * 1024 + 1024]
        A.op("vector", lambda e, xr=xr, qt=qt: e.reduce_sum(lnst[:, qt:qt + 1], xr, AX.X),
             waits=[xwait(qt)])
        A.op("vector", lambda e, qt=qt: e.reduce_sum(lnst[:, 16 + qt:17 + qt], sqt[qt % 2][:, :], AX.X),
             waits=[(g("aln_sq"), qt + 1)], incs=[(g("asum"), 1)])

    def emit_ln_a(gq):
        if gq < 2:
            for qt in range(4 * gq, 4 * gq + 4):
                emit_tile_sums(qt)
        base = 32 + 28 * (gq % 2)
        mean4 = lnst[:, base:base + 4]; t4 = lnst[:, base + 4:base + 8]
        msq4 = lnst[:, base + 8:base + 12]; vpe4 = lnst[:, base + 12:base + 16]
        sq4 = lnst[:, base + 16:base + 20]; r4 = lnst[:, base + 20:base + 24]
        negmr4 = lnst[:, base + 24:base + 28]
        A.op("vector", lambda e, gq=gq, o=mean4: e.tensor_scalar(o, lnst[:, 4 * gq:4 * gq + 4], 1.0 / 1024, None, OP.mult),
             waits=[(g("vch"), g("vch").n), (g("axn"), 4 * gq - 4)], incs=[(g("vch"), 1)])
        A.op("vector", lambda e, gq=gq, o=t4: e.tensor_scalar(o, lnst[:, 16 + 4 * gq:20 + 4 * gq], 1.0 / 1024, EPS, OP.mult, OP.add),
             waits=[(g("vch"), g("vch").n)], incs=[(g("vch"), 1)])
        A.op("vector", lambda e, i=mean4, o=msq4: e.tensor_tensor(o, i, i, OP.mult),
             waits=[(g("vch"), g("vch").n)], incs=[(g("vch"), 1)])
        A.op("vector", lambda e, a=t4, b=msq4, o=vpe4: e.tensor_tensor(o, a, b, OP.subtract),
             waits=[(g("vch"), g("vch").n)], incs=[(g("aln_v"), 1), (g("vch"), 1)])
        A.op("scalar", lambda e, i=vpe4, o=sq4: e.activation(o, i, AF.Sqrt),
             waits=[(g("aln_v"), 2 * gq + 1)], incs=[(g("aln_s"), 1)])
        A.op("vector", lambda e, i=sq4, o=r4: e.reciprocal(o, i),
             waits=[(g("aln_s"), gq + 1)], incs=[(g("vch"), 1)])
        A.op("vector", lambda e, i=mean4, r=r4, o=negmr4: e.scalar_tensor_tensor(
            o, i, -1.0, r, OP.mult, OP.mult),
            waits=[(g("vch"), g("vch").n)], incs=[(g("aln_v"), 1), (g("vch"), 1)])
        def emit_act(qt):
            xr = xring[(qt // 2) % 3][:, (qt % 2) * 1024:(qt % 2) * 1024 + 1024]
            j4c = qt - 4 * gq
            waits = [(g("aln_v"), 2 * gq + 2), (g("atp"), 2 * (qt - 1))]
            if qt == 2:
                waits.append((g("slt"), 16))
            A.op("scalar", lambda e, xr=xr, qt=qt, r=r4[:, j4c:j4c + 1], b=negmr4[:, j4c:j4c + 1]: e.activation(
                xntm[qt % 2][:, :], xr, AF.Identity, bias=b, scale=r),
                waits=waits, incs=[(g("axn"), 1)])
        for qt in range(4 * gq, 4 * gq + 2):
            emit_act(qt)
        acts_b[gq] = emit_act
        return

    acts_b = {}

    def emit_ln_b(gq):
        for qt in range(4 * gq + 2, 4 * gq + 4):
            acts_b[gq](qt)
        for qt in range(4 * gq, 4 * gq + 4):
            for b in range(2):
                for j4 in range(4):
                    j = b * 4 + j4
                    waits = []
                    if j == 0:
                        waits = [(g("axn"), qt + 1), (g("axnE"), 2 * (qt - 1))]
                        if qt == 0:
                            waits.append((g("smisc"), 144))
                    incs = [(g("atp"), 1)] if j4 == 3 else []
                    A.op("tensor", lambda e, qt=qt, b=b, j4=j4, j=j: e.transpose(
                        PA[(qt % 2) * 2 + b][:, j4 * 128:(j4 + 1) * 128],
                        xntm[qt % 2][:, j * 128:(j + 1) * 128], identB[:, :]),
                        waits=waits, incs=incs)
                dst = xn_FM[:, :].rearrange("p (c q) -> p c q", c=8)[:, b * 4:(b + 1) * 4, qt * 128:(qt + 1) * 128]
                src_fn = lambda e, qt=qt, b=b, dst=dst: e.tensor_copy(
                    dst, PA[(qt % 2) * 2 + b][:, :].rearrange("p (c q) -> p c q", c=4))
                A.op("vector", src_fn, waits=[(g("atp"), 2 * qt + b + 1)], incs=[(g("axnE"), 1)])

    def emit_qk(gq):
        nonlocal use
        for (wsem, wsb, dstT, bias) in [("swq", 0, Q_FM, bqB), ("swk", 1, K_FM, bkB)]:
            for c in range(8):
                for k in range(8):
                    waits = []
                    if k == 0:
                        waits = [(g("bev"), use - 3)]
                        if c == 0:
                            waits.append((g("axnE"), 8 * gq + 8))
                            waits.append((g(wsem), 16))
                    incs = [(g("bmm"), 1)] if k == 7 else []
                    A.op("tensor", lambda e, u=use, w=wsb, c=c, gq=gq, k=k: e.matmul(
                        PB[u % 4][:, :],
                        wbuf[w][:, k * 1024 + c * 128: k * 1024 + (c + 1) * 128],
                        xn_FM[:, k * 2048 + gq * 512: k * 2048 + (gq + 1) * 512],
                        start=(k == 0), stop=(k == 7)),
                        waits=waits, incs=incs)
                bw = [(g("bmm"), use + 1)]
                if use == 0:
                    bw.append((g("smisc"), 144))
                A.op("scalar", lambda e, u=use, dstT=dstT, bias=bias, c=c, gq=gq: e.activation(
                    dstT[:, c * 2048 + gq * 512: c * 2048 + (gq + 1) * 512],
                    PB[u % 4][:, :], AF.Identity, bias=bias[:, c:c + 1]),
                    waits=bw, incs=[(g("bev"), 1)])
                use += 1
                if wsb == 0 and c % 2 == 1 and gq + 2 < 4:
                    emit_tile_sums(4 * (gq + 2) + (c - 1) // 2)

    emit_ln_a(0)
    emit_ln_b(0)
    emit_ln_a(1)
    for gq in range(4):
        emit_qk(gq)
        if gq + 1 < 4:
            emit_ln_b(gq + 1)
        if gq + 2 < 4:
            emit_ln_a(gq + 2)
    A.op("gpsimd", lambda e: e.dma_start(
        wbuf[0][:, :].rearrange("p (c o) -> p c o", c=8),
        d_wv.rearrange("(c p) o -> p c o", p=128)),
        waits=[(g("bmm"), 56)], incs=[(g("swv"), 16)])
    A.op("gpsimd", lambda e: e.dma_start(
        wbuf[1][:, :].rearrange("p (c o) -> p c o", c=8),
        d_wo.rearrange("(c p) o -> p c o", p=128)),
        waits=[(g("bmm"), 64)], incs=[(g("swo"), 16)])
    for qt in range(NQT):
        for vh in range(2):
            waits = [(g("bev"), use - 3)]
            if qt == 0 and vh == 0:
                waits.append((g("swv"), 16))
                waits.append((g("sbias"), 48))
            A.op("tensor", lambda e, u=use, vh=vh: e.matmul(
                PB[u % 4][:, :], onescol[0:1, :], bvbf[0:1, vh * 512:(vh + 1) * 512],
                start=True, stop=False), waits=waits)
            for k in range(8):
                incs = [(g("bmm"), 1)] if k == 7 else []
                A.op("tensor", lambda e, u=use, qt=qt, vh=vh, k=k: e.matmul(
                    PB[u % 4][:, :],
                    xn_FM[:, k * 2048 + qt * 128: k * 2048 + (qt + 1) * 128],
                    wbuf[0][:, k * 1024 + vh * 512: k * 1024 + (vh + 1) * 512],
                    start=False, stop=(k == 7)), incs=incs)
            dst = V_pad[:, qt * 1040 + vh * 8 * 65: qt * 1040 + (vh + 1) * 8 * 65].rearrange(
                "p (h s) -> p h s", h=8)[:, :, 0:64]
            A.op("scalar", lambda e, u=use, dst=dst: e.activation(
                dst, PB[u % 4][:, :].rearrange("p (h s) -> p h s", h=8), AF.Identity),
                waits=[(g("bmm"), use + 1)], incs=[(g("bev"), 1)])
            use += 1

    A.op("sync", lambda e: e.dma_start(d_xnd[:, 0:4096], xn_FM[:, 0:4096]),
         waits=[(g("axnE"), 32)], incs=[(g("sxnsp"), 16)])
    A.op("scalar", lambda e: e.dma_start(d_xnd[:, 4096:8192], xn_FM[:, 4096:8192]),
         waits=[(g("axnE"), 32)], incs=[(g("sxnsp2"), 16)])
    A.op("gpsimd", lambda e: e.dma_start(d_xnd[:, 12288:16384], xn_FM[:, 12288:16384]),
         waits=[(g("axnE"), 32)], incs=[(g("sxnsp4"), 16)])
    A.op("gpsimd", lambda e: e.dma_start(d_xnd[:, 8192:12288], xn_FM[:, 8192:12288]),
         incs=[(g("sxnsp3"), 16)])
    A.op("vector", lambda e: e.memset(kaug[0][64:65, :], 1.0), waits=[(g("sxnsp2"), 16)])
    A.op("vector", lambda e: e.memset(kaug[1][64:65, :], 1.0))
    A.op("vector", lambda e: e.memset(zcol[:, :], 0.0), waits=[(g("sxnsp4"), 16)])

    C = Plan()
    C.op("gpsimd", lambda e: e.dma_start(amask[:, :], d_amk[:, :]), incs=[(g("samk"), 16)])

    slot_consumer = []
    csc_after = []
    nexp = [0]
    nmax = [0]

    def slot_producer_waits(s):
        w = []
        if s >= 4:
            w.append((g(slot_consumer[s - 4][0]), slot_consumer[s - 4][1]))
        return w

    exp_after = {}
    csc_after_head = {}

    def emit_diag_slots(hh, s0):
        c2, r0 = hh >> 1, 64 * (hh & 1)
        for half in range(4):
            s = s0 + half
            w0 = slot_producer_waits(s)
            for j in range(4):
                qt = 4 * half + j
                waits = list(w0) if j == 0 else []
                incs = [(g("csc"), 1)] if j == 3 else []
                C.op("tensor", lambda e, c2=c2, r0=r0, qt=qt, j=j, s=s: e.matmul(
                    PSC[s % 4][:, j * 128:(j + 1) * 128],
                    Q_FM[r0:r0 + 64, c2 * 2048 + qt * 128: c2 * 2048 + (qt + 1) * 128],
                    K_FM[r0:r0 + 64, c2 * 2048 + qt * 128: c2 * 2048 + (qt + 1) * 128],
                    start=True, stop=True), waits=waits, incs=incs)
            csc_after.append(g("csc").n)
            C.op("vector", lambda e, half=half, s=s: e.reduce_max(
                m_all[:, half * 4:(half + 1) * 4],
                PSC[s % 4][:, :].rearrange("p (a q) -> p a q", a=4), AX.X),
                waits=[(g("csc"), csc_after[s])], incs=[(g("cmax"), 1)])
            nmax[0] += 1
            slot_consumer.append(("cmax", nmax[0]))

    def emit_vt_slot(hh, s):
        w0 = slot_producer_waits(s)
        for gg in range(4):
            waits = list(w0) + [(g("cvb"), hh + 1)] if gg == 0 else []
            incs = [(g("csc"), 1), (g("cvt"), 1)] if gg == 3 else []
            C.op("tensor", lambda e, gg=gg, s=s: e.matmul(
                PSC[s % 4][0:4, gg * 128:(gg + 1) * 128],
                vb[:, :].rearrange("p (m q) -> p q m", q=4)[:, gg, :], identB[:, :],
                start=True, stop=True), waits=waits, incs=incs)
        csc_after.append(g("csc").n)
        waits = [(g("csc"), csc_after[s])]
        if hh >= 1:
            waits.append((g("saugv%d" % ((hh - 1) % 2)), 16 * ((hh - 1) // 2 + 1)))
        C.op("vector", lambda e, s=s: e.tensor_copy(vT_sb[:, :], PSC[s % 4][0:4, 0:512]),
             waits=waits, incs=[(g("cmax"), 1), (g("cvtE"), 1)])
        nmax[0] += 1
        slot_consumer.append(("cmax", nmax[0]))

    def emit_av(oi, h, kb):
        spans, chunks, kmin, kmax = SCHED[h]
        qend = spans[kb][1]
        pt = P_T[(16 * oi + kb) % 5]
        qgs = [qg for qg in range(kb // 4, 4) if qg * 512 < qend]
        last_qg = qgs[-1]
        for qi, qg in enumerate(qgs):
            q0 = max(kb * 128, qg * 512)
            q1 = min(qend, (qg + 1) * 512)
            waits = []
            if qi == 0:
                waits.append((g("cexp"), exp_after[(h, kb)]))
                waits.append((g("cmsk"), oi * 16 + kb + 1))
            if kb == kmin[qg]:
                zw = [(g("cacc"), 4 * (oi - 1) + qg + 1), (g("crec"), 4 * (oi - 1) + qg + 1),
                      (g("catt"), 4 * (oi - 1) + qg + 1)]
                C.op("tensor", lambda e, qg=qg: e.matmul(
                    PAV[qg][0:65, 0:512], zcol[:, 0:65], V_pad[:, 0:512],
                    start=True, stop=False), waits=zw)
            stop = (kb == kmax[qg])
            incs = []
            if stop:
                incs.append((g("cavq"), 1))
            if qg == last_qg:
                incs.append((g("cavk"), 1))
            C.op("tensor", lambda e, h=h, kb=kb, qg=qg, q0=q0, q1=q1, stop=stop, pt=pt: e.matmul(
                PAV[qg][0:65, q0 - qg * 512: q1 - qg * 512],
                V_pad[:, kb * 1040 + h * 65: kb * 1040 + h * 65 + 65],
                pt[:, q0:q1], start=False, stop=stop),
                waits=waits, incs=incs)

    def emit_vb(oi, hh):
        vbw = [(g("cvt"), oi), (g("vch"), 0)]
        if oi == 0:
            vbw.append((g("smisc"), 144))
        C.op("vector", lambda e, hh=hh: e.tensor_scalar(vbtmp[:, :], qpos[:, :], -8.0 * SLOPES[hh], None, OP.mult),
             waits=vbw, incs=[(g("vch"), 1)])
        C.op("vector", lambda e: e.tensor_tensor(vb[:, :], vbtmp[:, :], m_all[:, :], OP.subtract),
             waits=[(g("vch"), g("vch").n), (g("cmax"), nmax[0])], incs=[(g("cvb"), 1)])

    pending_rn = []
    pending_pre = []

    def emit_renorm_pre(oi, qg):
        C.op("scalar", lambda e, qg=qg: e.activation(lnrow[0:1, :], PAV[qg][64:65, :], AF.Ln),
             waits=[(g("cavq"), 4 * oi + qg + 1), (g("crep"), 4 * (oi - 1) + qg + 1)])
        C.op("scalar", lambda e, qg=qg: e.activation(
            recB[0:1, qg * 512:(qg + 1) * 512], lnrow[0:1, :], AF.Exp, scale=-1.0),
            incs=[(g("crec"), 1)])

    def emit_renorm_copy(oi, qg):
        C.op("vector", lambda e, qg=qg: e.tensor_copy(
            accS[qg % 2][:, :], PAV[qg][0:64, :]),
            waits=[(g("cavq"), 4 * oi + qg + 1), (g("catt"), 4 * oi + qg - 1)],
            incs=[(g("cacc"), 1)])

    def emit_renorm_qg(oi, c2, r0, qg):
        if qg >= 2:
            emit_renorm_copy(oi, qg)
        C.op("tensor", lambda e, qg=qg: e.matmul(
            PAV[qg][0:64, 0:512], onescol[0:1, 0:64],
            recB[0:1, qg * 512:(qg + 1) * 512],
            start=True, stop=True),
            waits=[(g("crec"), 4 * oi + qg + 1), (g("cacc"), 4 * oi + qg + 1)],
            incs=[(g("crep"), 1)])
        C.op("vector", lambda e, c2=c2, r0=r0, qg=qg: e.tensor_tensor(
            attnFM[r0:r0 + 64, c2 * 2048 + qg * 512: c2 * 2048 + (qg + 1) * 512],
            accS[qg % 2][:, :], PAV[qg][0:64, 0:512], OP.mult),
            waits=[(g("crep"), 4 * oi + qg + 1), (g("cacc"), 4 * oi + qg + 1)],
            incs=[(g("catt"), 1)])

    HORD = list(range(nheads))
    pos_of = {hh: oo for oo, hh in enumerate(HORD)}

    for oi, h in enumerate(HORD):
        c2, r0 = h >> 1, 64 * (h & 1)
        spans, chunks, kmin, kmax = SCHED[h]
        if oi == 0:
            emit_diag_slots(0, 0)
            emit_vb(0, 0)
            emit_vt_slot(0, 4)
        waits = [(g("sxnsp"), 16)]
        if oi >= 2:
            waits.append((g("csc"), csc_after_head[oi - 2]))
        C.op("sync", lambda e, oi=oi, c2=c2, r0=r0: e.dma_start(
            qaug[oi % 2][0:64, :], Q_FM[r0:r0 + 64, c2 * 2048:(c2 + 1) * 2048]),
            waits=waits, incs=[(g("saugq%d" % (oi % 2)), 16)])
        kgw = [(g("sxnsp2"), 16)] if oi == 0 else []
        C.op("sync", lambda e, oi=oi, c2=c2, r0=r0: e.dma_start(
            kaug[oi % 2][0:64, :], K_FM[r0:r0 + 64, c2 * 2048:(c2 + 1) * 2048]),
            waits=kgw, incs=[(g("saugk%d" % (oi % 2)), 16)])
        C.op("sync", lambda e, oi=oi: e.dma_start(
            qaug[oi % 2][64:65, :].rearrange("p (m r) -> p m r", m=4),
            vT_sb[:, :]),
            waits=[(g("cvtE"), oi + 1)], incs=[(g("saugv%d" % (oi % 2)), 16)])
        pc = [len(chunks[kb_]) for kb_ in range(NKB)]
        tot_pc = sum(pc)
        kb_d, kb_e = NKB - 4, NKB - 3
        cum = 0
        for kb_ in range(NKB):
            cum += pc[kb_]
            if tot_pc - cum <= 14 and kb_d == NKB - 4:
                kb_d = max(1, min(kb_, NKB - 4))
            if tot_pc - cum <= 8:
                kb_e = max(kb_d + 1, min(kb_, NKB - 3))
                break
        for kb in range(NKB):
            pt = P_T[(16 * oi + kb) % 5]
            q0span, qend = spans[kb]
            pieces = []
            p0 = q0span
            while p0 < qend:
                p1 = min(qend, (p0 // 512 + 1) * 512)
                pieces.append((p0, p1))
                p0 = p1
            for pi, (p0, p1) in enumerate(pieces):
                s = len(slot_consumer)
                w0 = slot_producer_waits(s)
                if kb == 0 and pi == 0:
                    w0.append((g("saugq%d" % (oi % 2)), 16 * (oi // 2 + 1)))
                    w0.append((g("saugk%d" % (oi % 2)), 16 * (oi // 2 + 1)))
                    w0.append((g("saugv%d" % (oi % 2)), 16 * (oi // 2 + 1)))
                subs = []
                c0s = p0
                while c0s < p1:
                    c1s = min(p1, c0s + 512)
                    subs.append((c0s, c1s))
                    c0s = c1s
                for si, (c0s, c1s) in enumerate(subs):
                    waits = list(w0) if si == 0 else []
                    incs = [(g("csc"), 1)] if si == len(subs) - 1 else []
                    C.op("tensor", lambda e, oi=oi, kb=kb, c0s=c0s, c1s=c1s, p0=p0, s=s: e.matmul(
                        PSC[s % 4][:, c0s - p0: c1s - p0],
                        kaug[oi % 2][:, kb * 128:(kb + 1) * 128],
                        qaug[oi % 2][:, c0s:c1s], start=True, stop=True),
                        waits=waits, incs=incs)
                csc_after.append(g("csc").n)
                if h == 0 and pi == 0:
                    C.op("vector", lambda e, s=s, p0=p0, kb=kb: e.tensor_tensor(
                        PSC[s % 4][:, kb * 128 - p0: (kb + 1) * 128 - p0],
                        PSC[s % 4][:, kb * 128 - p0: (kb + 1) * 128 - p0], amask[:, :], OP.add),
                        waits=[(g("csc"), csc_after[s]), (g("samk"), 16)], incs=[(g("cpm"), 1)])
                waits = [(g("csc"), csc_after[s]), (g("cavk"), oi * 16 + kb - 4),
                         (g("cmsk"), oi * 16 + kb - 4)]
                if h == 0 and pi == 0:
                    waits.append((g("cpm"), kb + 1))
                if h == 0 and kb == 0 and pi == 0:
                    waits.append((g("sxnsp3"), 16))
                    waits.append((g("sxnsp4"), 16))
                    waits.append((g("smisc"), 144))
                C.op("scalar", lambda e, h=h, kb=kb, p0=p0, p1=p1, s=s, pt=pt: e.activation(
                    pt[:, p0:p1], PSC[s % 4][:, 0:p1 - p0], AF.Exp,
                    bias=alibi[:, h * 16 + kb: h * 16 + kb + 1], scale=0.125),
                    waits=waits, incs=[(g("cexp"), 1)])
                nexp[0] += 1
                slot_consumer.append(("cexp", nexp[0]))
                if pi == 0:
                    C.op("vector", lambda e, kb=kb, pt=pt: e.tensor_tensor(
                        pt[:, kb * 128:(kb + 1) * 128],
                        pt[:, kb * 128:(kb + 1) * 128], minmask[:, :], OP.min),
                        waits=[(g("cexp"), nexp[0])], incs=[(g("cmsk"), 1)])
                if pending_pre:
                    emit_renorm_pre(*pending_pre.pop(0))
                if pending_rn:
                    emit_renorm_qg(*pending_rn.pop(0))
            exp_after[(h, kb)] = nexp[0]
            if h == 0 and kb == 0:
                pass
            if kb == kb_d and oi + 1 < nheads:
                emit_diag_slots(HORD[oi + 1], len(slot_consumer))
                emit_vb(oi + 1, HORD[oi + 1])
            if kb == kb_e and oi + 1 < nheads:
                emit_vt_slot(oi + 1, len(slot_consumer))
            if kb >= 2:
                while pending_pre:
                    emit_renorm_pre(*pending_pre.pop(0))
                while pending_rn:
                    emit_renorm_qg(*pending_rn.pop(0))
                emit_av(oi, h, kb - 2)
            if kb == NKB - 1:
                csc_after_head[oi] = g("csc").n
                emit_av(oi, h, kb - 1)
                emit_av(oi, h, kb)
        while pending_pre:
            emit_renorm_pre(*pending_pre.pop(0))
        emit_renorm_pre(oi, 0)
        for qg in range(2):
            emit_renorm_copy(oi, qg)
        if oi + 1 < nheads:
            if sum(len(c) for c in SCHED[HORD[oi + 1]][1]) >= 12:
                pending_pre.extend((oi, qg) for qg in range(1, 4))
            else:
                for qg in range(1, 4):
                    emit_renorm_pre(oi, qg)
            pending_rn.extend((oi, c2, r0, qg) for qg in range(4))
        else:
            for qg in range(1, 4):
                emit_renorm_pre(oi, qg)
            for qg in range(4):
                emit_renorm_qg(oi, c2, r0, qg)

    for i in range(4):
        gate = csc_after_head[max(pos_of[hh] for hh in range(4 * i, min(4 * i + 4, nheads)))]
        for half in range(2):
            c = i if half == 0 else i + 4
            C.op("gpsimd", lambda e, c=c: e.dma_start(
                w1t_sb[:, c * 4096:(c + 1) * 4096], d_w1t[c * 128:(c + 1) * 128, :]),
                waits=[(g("csc"), gate)], incs=[(g("sw1"), 16)])

    Dp = Plan()
    _dbg_tensors = dict(qaug0=qaug[0], kaug0=kaug[0], m_all=m_all, vb=vb, vT_sb=vT_sb,
                        attnFM=attnFM, pt0=P_T[0], pt1=P_T[1], recB=recB,
                        V_pad=V_pad, Q_FM=Q_FM, K_FM=K_FM, accS0=accS[0], accS1=accS[1])
    if dumps:
        SM["sdump"] = Sem(nc, "sdump")
    for _dn in dumps:
        _t = _dbg_tensors[_dn]
        _dd = nc.dram_tensor("dump_" + _dn, list(_t.shape), _t.dtype, kind="ExternalOutput").ap()
        Dp.op("sync", lambda e, a=_dd, b=_t: e.dma_start(a[:, :], b[:, :]), incs=[(g("sdump"), 16)])
    if dumps:
        Dp.op("sync", lambda e: e.wait_ge(g("sdump").h, 16 * len(dumps)))

    for ci in range(8):
        w2w = [(g("dmm"), 32)] if ci >= 6 else []
        w2sem = "sw2b" if ci >= 6 else "sw2"
        Dp.op("gpsimd", lambda e, ci=ci: e.dma_start(
            w2t_sb[:, ci * 4096:(ci + 1) * 4096].rearrange("p (q o) -> p q o", q=4),
            d_w2t[ci * 512:(ci + 1) * 512, :].rearrange("(q p) o -> p q o", p=128)),
            waits=w2w, incs=[(g(w2sem), 16)])

    Dp.op("sync", lambda e: e.dma_start(selmat[:, :], d_sel[:, :]), incs=[(g("ssel"), 16)])
    Dp.op("sync", lambda e: e.dma_start(
        xnring[0][:, :].rearrange("p (c q) -> p c q", c=8),
        d_xnd[:, :].rearrange("p (c q) -> p c q", c=8)[:, :, 0:256]),
        waits=[(g("sxnsp2"), 16), (g("sxnsp3"), 16), (g("sxnsp4"), 16)], incs=[(g("sxnr0"), 16)])

    Dp.op("vector", lambda e: e.tensor_scalar(scalebf[0:1, 0:1], sclsb[0:1, 0:1], 1.0, None, OP.mult),
         waits=[(g("smisc"), 144)], incs=[(g("d2sc"), 1)])
    for c in range(8):
        Dp.op("tensor", lambda e, c=c: e.matmul(
            PE2[:, c * 16:(c + 1) * 16], lora[:, c * 128:(c + 1) * 128], identB[0:16, 0:16],
            start=True, stop=True))
    Dp.op("tensor", lambda e: e.matmul(
        PE2[:, 128:129], onescol[0:1, :], scalebf[0:1, 0:1], start=True, stop=True),
        waits=[(g("d2sc"), 1)], incs=[(g("d2oc"), 1)])
    Dp.op("vector", lambda e: e.tensor_copy(scale_rep[:, 0:1], PE2[:, 128:129]),
         waits=[(g("d2oc"), 1)], incs=[(g("vch"), 1)])
    Dp.op("vector", lambda e: e.tensor_scalar(oddcT[:, :], PE2[:, 0:128], scale_rep[:, 0:1], None, OP.mult),
         waits=[(g("vch"), g("vch").n)], incs=[(g("d2ocE"), 1), (g("vch"), 1)])

    w1_half = [0]
    def emit_w1_half():
        i = w1_half[0]
        if i >= 8:
            return
        j, hf = i // 2, i % 2
        waits = [(g("ssel"), 16), (g("d2ocE"), 1)]
        if i >= 2:
            waits.append((g("d2ev"), i - 1))
        Dp.op("tensor", lambda e, j=j, hf=hf, i=i: e.matmul(
            PE2[:, (i % 2) * 512:(i % 2) * 512 + 512],
            selmat[:, 2 * j * 128:(2 * j + 1) * 128], lora[:, hf * 512:(hf + 1) * 512],
            start=True, stop=True), waits=waits, incs=[(g("d2er"), 1)])
        ew = [(g("d2er"), i + 1)]
        if i >= 2:
            ew.append((g("d2erE"), i - 1))
        Dp.op("scalar", lambda e, i=i: e.activation(
            evr1[i % 2][:, :], PE2[:, (i % 2) * 512:(i % 2) * 512 + 512], AF.Identity),
            waits=ew, incs=[(g("d2ev"), 1)])
        for c in range(8):
            incs = []
            if c == 7:
                incs.append((g("d2erE"), 1))
                if hf == 1:
                    incs.append((g("d2f"), 1))
            w1w = [(g("sw1"), 128)]
            if c == 0:
                w1w.append((g("d2ev"), i + 1))
            Dp.op("vector", lambda e, j=j, hf=hf, c=c, i=i: e.scalar_tensor_tensor(
                w1t_sb[:, c * 4096 + j * 1024 + hf * 512: c * 4096 + j * 1024 + hf * 512 + 512],
                evr1[i % 2][:, :],
                oddcT[:, c * 16 + 1 + 2 * j: c * 16 + 2 + 2 * j],
                w1t_sb[:, c * 4096 + j * 1024 + hf * 512: c * 4096 + j * 1024 + hf * 512 + 512],
                OP.mult, OP.add),
                waits=w1w, incs=incs)
        w1_half[0] += 1

    duse = 0
    for qt in range(NQT):
        for oh in range(2):
            for c in range(8):
                waits = []
                if c == 0:
                    waits = [(g("dev"), duse - 3)]
                    if duse == 0:
                        waits.append((g("swo"), 16))
                incs = [(g("dmm"), 1)] if c == 7 else []
                Dp.op("tensor", lambda e, u=duse, qt=qt, oh=oh, c=c: e.matmul(
                    POUT[u % 4][:, :],
                    attnFM[:, c * 2048 + qt * 128: c * 2048 + (qt + 1) * 128],
                    wbuf[1][:, c * 1024 + oh * 512: c * 1024 + (oh + 1) * 512],
                    start=(c == 0), stop=(c == 7)), waits=waits, incs=incs)
            dw = [(g("dmm"), duse + 1)]
            if qt >= 1:
                dw.append((g("sdst%d" % ((qt - 1) % 2)), 32 * ((qt - 1) // 2) + 16 * (oh + 1)))
            Dp.op("scalar", lambda e, u=duse, oh=oh: e.activation(
                dstage[:, oh * 512:(oh + 1) * 512], POUT[u % 4][:, :], AF.Identity),
                waits=dw, incs=[(g("dev"), 1)])
            duse += 1
            Dp.op("sync", lambda e, qt=qt, oh=oh: e.dma_start(
                d_std[:, qt * 1024 + oh * 512: qt * 1024 + (oh + 1) * 512],
                dstage[:, oh * 512:(oh + 1) * 512]),
                waits=[(g("dev"), 2 * qt + oh + 1)], incs=[(g("sdst%d" % (qt % 2)), 16)])
        if qt % 2 == 1:
            emit_w1_half()
    while w1_half[0] < 8:
        emit_w1_half()

    w2_group = [0]
    def emit_w2_group():
        gg = w2_group[0]
        if gg >= 4:
            return
        for a in range(4):
            er = 8 + 2 * a
            k = gg * 4 + a
            for hf in range(2):
                waits = []
                if hf == 0:
                    waits = [(g("d2evw2"), k)] if k >= 1 else [(g("d2ev"), 8)]
                incs = [(g("d2er"), 1)] if hf == 1 else []
                Dp.op("tensor", lambda e, er=er, hf=hf, waits=waits: e.matmul(
                    PE2[:, hf * 512:(hf + 1) * 512],
                    selmat[:, er * 128:(er + 1) * 128], lora[:, hf * 512:(hf + 1) * 512],
                    start=True, stop=True), waits=waits, incs=incs)
            ew = [(g("d2er"), 9 + k), (g("d2erE"), 8 + max(k - 1, 0))]
            Dp.op("scalar", lambda e, gg=gg, k=k: e.activation(
                evr2[k % 2][:, :],
                PE2[:, 0:1024].rearrange("p (o e) -> p e o", e=4)[:, gg, :],
                AF.Identity), waits=ew, incs=[(g("d2evw2"), 1)])
            for c2 in range(8):
                ch = gg * 8 + c2
                incs = []
                if c2 == 7:
                    incs.append((g("d2erE"), 1))
                    incs.append((g("d2fw2"), 1))
                ws = [(g("sw2"), 96)] if gg < 3 else [(g("sw2b"), 32)]
                if c2 == 0:
                    ws.append((g("d2evw2"), k + 1))
                Dp.op("vector", lambda e, a=a, gg=gg, c2=c2, ch=ch, k=k: e.scalar_tensor_tensor(
                    w2t_sb[:, ch * 1024 + a * 256: ch * 1024 + (a + 1) * 256],
                    evr2[k % 2][:, :],
                    oddcT[:, c2 * 16 + 9 + 2 * a: c2 * 16 + 10 + 2 * a],
                    w2t_sb[:, ch * 1024 + a * 256: ch * 1024 + (a + 1) * 256],
                    OP.mult, OP.add),
                    waits=ws, incs=incs)
        w2_group[0] += 1

    def emit_ring_loads(tg):
        for qt2 in range(2):
            qt = 2 * tg + qt2
            waits = [(g("sdst%d" % (qt % 2)), 32 * (qt // 2 + 1))]
            if tg >= 1:
                waits.append((g("esd"), tg))
            Dp.op("sync", lambda e, qt=qt, qt2=qt2: e.dma_start(
                sgring[qt2][:, :], d_std[:, qt * 1024:(qt + 1) * 1024]),
                waits=waits, incs=[(g("ssgr%d" % qt2), 16)])
        if tg >= 1:
            waits = [(g("ep1"), 32 * (tg - 1))] if tg >= 2 else []
            Dp.op("sync", lambda e, tg=tg: e.dma_start(
                xnring[tg % 2][:, :].rearrange("p (c q) -> p c q", c=8),
                d_xnd[:, :].rearrange("p (c q) -> p c q", c=8)[:, :, tg * 256:(tg + 1) * 256]),
                waits=waits, incs=[(g("sxnr%d" % (tg % 2)), 16)])
        for qt2 in range(2):
            qt = 2 * tg + qt2
            waits = [(g("epf"), 2 * (qt - 1))] if qt >= 2 else [(g("dmm"), 32)]
            Dp.op("sync", lambda e, qt=qt: e.dma_start(
                xres[qt % 2][:, :], d_x[qt * 128:(qt + 1) * 128, :]),
                waits=waits, incs=[(g("sxre%d" % (qt % 2)), 16)])

    for tg in range(8):
        if tg == 0:
            emit_ring_loads(0)
        def emit_seeds(tg=tg):
            for qt2 in range(2):
                for oh in range(2):
                    b = qt2 * 2 + oh
                    waits = [(g("ssgr%d" % qt2), 16 * (tg + 1))]
                    if tg >= 1:
                        waits.append((g("epf"), 4 * (tg - 1) + b + 1))
                    else:
                        waits.append((g("dev"), 29 + b))
                    Dp.op("tensor", lambda e, b=b, qt2=qt2, oh=oh: e.matmul(
                        POUT[b][:, :], identB[:, :], sgring[qt2][:, oh * 512:(oh + 1) * 512],
                        start=True, stop=False), waits=waits)
                    waits2 = [(g("sbias"), 48)] if (tg == 0 and b == 0) else []
                    incs = [(g("esd"), 1)] if b == 3 else []
                    Dp.op("tensor", lambda e, b=b, oh=oh: e.matmul(
                        POUT[b][:, :], ones2[0:2, :], bb2[0:2, oh * 512:(oh + 1) * 512],
                        start=False, stop=False), waits=waits2, incs=incs)
        def emit_p2(m, tg=tg):
            for qt2 in range(2):
                for oh in range(2):
                    b = qt2 * 2 + oh
                    waits = []
                    if qt2 == 0 and oh == 0:
                        waits = [(g("egel"), 32 * tg + m + 1)]
                        if tg == 0:
                            waits.append((g("d2fw2"), 4 * (m // 8) + 4))
                    incs = [(g("ep2"), 1)] if b == 3 else []
                    Dp.op("tensor", lambda e, m=m, qt2=qt2, oh=oh, b=b: e.matmul(
                        POUT[b][:, :],
                        hgel[:, m * 256 + qt2 * 128: m * 256 + qt2 * 128 + 128],
                        w2t_sb[:, m * 1024 + oh * 512: m * 1024 + (oh + 1) * 512],
                        start=False, stop=(m == 31)), waits=waits, incs=incs)
        for m in range(32):
            for k in range(8):
                waits = []
                if k == 0:
                    waits = [(g("egel"), 32 * tg + m - 1)]
                    if m == 0:
                        waits.append((g("sxnr%d" % (tg % 2)), 16 * (tg // 2 + 1)))
                    if tg == 0:
                        waits.append((g("d2f"), m // 8 + 1))
                        if m == 0:
                            waits.append((g("sw1"), 128))
                incs = [(g("ep1"), 1)] if k == 7 else []
                Dp.op("tensor", lambda e, tg=tg, m=m, k=k: e.matmul(
                    PH[m % 2][:, :],
                    w1t_sb[:, k * 4096 + m * 128: k * 4096 + (m + 1) * 128],
                    xnring[tg % 2][:, k * 256:(k + 1) * 256],
                    start=(k == 0), stop=(k == 7)), waits=waits, incs=incs)
            if tg == 0 and m in (1, 9, 17, 25):
                emit_w2_group()
            if m == 2:
                emit_seeds()
            if m >= 3:
                emit_p2(m - 3)
            gw = [(g("ep1"), 32 * tg + m + 1), (g("ep2"), 32 * (tg - 1) + m + 1)]
            Dp.op("scalar", lambda e, tg=tg, m=m: e.activation(
                hgel[:, m * 256:(m + 1) * 256], PH[m % 2][:, :], AF.Gelu, bias=b1B[:, m:m + 1]),
                waits=gw, incs=[(g("egel"), 1)])
        if tg + 1 < 8:
            emit_ring_loads(tg + 1)
        for mm_ in (29, 30, 31):
            emit_p2(mm_)
        for qt2 in range(2):
            qt = 2 * tg + qt2
            for oh in range(2):
                b = qt2 * 2 + oh
                waits = [(g("ep2"), 32 * tg + 32), (g("sxre%d" % (qt % 2)), 16 * (qt // 2 + 1))]
                if oh == 0 and qt >= 2:
                    waits.append((g("sout%d" % (qt % 2)), 16 * (qt // 2)))
                Dp.op("vector", lambda e, qt=qt, oh=oh, b=b: e.tensor_tensor(
                    outsb[qt % 2][:, oh * 512:(oh + 1) * 512], POUT[b][:, :],
                    xres[qt % 2][:, oh * 512:(oh + 1) * 512], OP.add),
                    waits=waits, incs=[(g("epf"), 1)])
            Dp.op("sync", lambda e, qt=qt: e.dma_start(
                d_out[qt * 128:(qt + 1) * 128, :], outsb[qt % 2][:, :]),
                waits=[(g("epf"), 2 * (qt + 1))], incs=[(g("sout%d" % (qt % 2)), 16)])
    Dp.op("sync", lambda e: e.wait_ge(g("sout0").h, 16 * 8))
    Dp.op("sync", lambda e: e.wait_ge(g("sout1").h, 16 * 8))

    import contextlib
    with contextlib.ExitStack() as es:
        PA = [es.enter_context(nc.psum_tensor(f"pa{i}", [128, 512], BF16)).ap() for i in range(4)]
        PB = [es.enter_context(nc.psum_tensor(f"pb{i}", [128, 512], F32)).ap() for i in range(4)]
        A.emit(nc)
    with contextlib.ExitStack() as es:
        PSC = [es.enter_context(nc.psum_tensor(f"psc{i}", [128, 512], F32)).ap() for i in range(4)]
        PAV = [es.enter_context(nc.psum_tensor(f"pav{i}", [128, 512], F32)).ap() for i in range(4)]
        C.emit(nc)
    with contextlib.ExitStack() as es:
        PE2 = es.enter_context(nc.psum_tensor("pe2", [128, 1024], F32)).ap()
        PH = [es.enter_context(nc.psum_tensor(f"ph{i}", [128, 256], F32)).ap() for i in range(2)]
        POUT = [es.enter_context(nc.psum_tensor(f"pout{i}", [128, 512], F32)).ap() for i in range(4)]
        Dp.emit(nc)
    return nc


def make_consts():
    ki = np.arange(128, dtype=np.float64)
    cols = []
    for h in range(H):
        for kb in range(NKB):
            cols.append(SLOPES[h] * (kb * 128 + ki))
    alibi = np.stack(cols, axis=1).astype(np.float32)
    qpos = np.stack([qt * 128 + ki for qt in range(16)], axis=1).astype(np.float32)
    identB = np.eye(128, dtype=np.float32).astype(NPBF)
    kk, qq = np.meshgrid(np.arange(128), np.arange(128), indexing="ij")
    minmask = np.where(kk <= qq, 1e38, 0.0).astype(np.float32).astype(NPBF)
    amask = np.where(kk <= qq, 0.0, -240000.0).astype(np.float32)
    onescol = np.ones((1, 128), dtype=np.float32).astype(NPBF)
    return alibi, qpos, identB, minmask, amask, onescol


_NC_CACHE = {}

def kernel(L0_attn_x, L0_attn_Wq, L0_attn_bq, L0_attn_Wk, L0_attn_bk, L0_attn_Wv,
           L0_attn_bv, L0_attn_Wo, L0_attn_bo, L0_ffn_W1, L0_ffn_b1, L0_ffn_W2,
           L0_ffn_b2, L0_ffn_scale):
    x = np.ascontiguousarray(np.asarray(L0_attn_x, dtype=np.float32))
    wq = np.ascontiguousarray(np.asarray(L0_attn_Wq, dtype=np.float32))
    wk = np.ascontiguousarray(np.asarray(L0_attn_Wk, dtype=np.float32))
    wv = np.ascontiguousarray(np.asarray(L0_attn_Wv, dtype=np.float32))
    wo = np.ascontiguousarray(np.asarray(L0_attn_Wo, dtype=np.float32))
    w1t = np.ascontiguousarray(np.asarray(L0_ffn_W1, dtype=np.float32).T)
    w2t = np.ascontiguousarray(np.asarray(L0_ffn_W2, dtype=np.float32).T)
    bqB = np.ascontiguousarray(np.asarray(L0_attn_bq, dtype=np.float32).reshape(8, 128).T)
    bkB = np.ascontiguousarray(np.asarray(L0_attn_bk, dtype=np.float32).reshape(8, 128).T)
    b1B = np.ascontiguousarray(np.asarray(L0_ffn_b1, dtype=np.float32).reshape(32, 128).T)
    bv = np.ascontiguousarray(np.asarray(L0_attn_bv, dtype=np.float32).reshape(1, D))
    bo = np.ascontiguousarray(np.asarray(L0_attn_bo, dtype=np.float32).reshape(1, D))
    b2 = np.ascontiguousarray(np.asarray(L0_ffn_b2, dtype=np.float32).reshape(1, D))
    scl = np.zeros((1, 8), dtype=np.float32)
    scl[0, 0] = np.asarray(L0_ffn_scale, dtype=np.float32)
    alibi, qpos, identB, minmask, amask, onescol = make_consts()

    nc = build_nc()
    selmat = np.zeros((16, 2048), dtype=np.float32)
    for r in range(16):
        selmat[r, r * 128:(r + 1) * 128] = 1.0
    selmat = selmat.astype(NPBF)
    common = dict(wq=wq, wk=wk, wv=wv, wo=wo, w1t=w1t, w2t=w2t, bqB=bqB, bkB=bkB,
                  b1B=b1B, bv=bv, bo=bo, b2=b2, scl=scl, alibi=alibi, qpos=qpos,
                  identb=identB, minmask=minmask, amask=amask, onescol=onescol, selmat=selmat)
    in_maps = [dict(common, x=np.ascontiguousarray(x[b])) for b in range(8)]
    res = run_bass_kernel_spmd(nc, in_maps, core_ids=list(range(8)))
    return np.stack([np.asarray(res.results[b]["out"]) for b in range(8)], axis=0)




# version 51
# speedup vs baseline: 1.0005x; 1.0005x over previous
import numpy as np
import ml_dtypes
import concourse.bass as bass
import concourse.mybir as mybir
from concourse.bass_utils import run_bass_kernel_spmd

F32 = mybir.dt.float32
BF16 = mybir.dt.bfloat16
AF = mybir.ActivationFunctionType
OP = mybir.AluOpType
AX = mybir.AxisListType
NPBF = ml_dtypes.bfloat16

S, D, H, HD, FH = 2048, 1024, 16, 64, 4096
NQT, NKB = 16, 16
EPS = 1e-5
SLOPES = [2.0 ** (-0.5 * (h + 1)) for h in range(H)]
DMAX = [min(15, int((65.0 / SLOPES[h] + 127.0) // 128)) for h in range(H)]

def head_sched(h):
    dm = DMAX[h]
    spans = []
    chunks = []
    for kb in range(NKB):
        q0 = kb * 128
        qend = min(S, 128 * (kb + dm + 1))
        spans.append((q0, qend))
        ch = []
        c0 = q0
        while c0 < qend:
            c1 = min(qend, (c0 // 512 + 1) * 512)
            ch.append((c0, c1))
            c0 = c1
        chunks.append(ch)
    kmin = [max(0, 4 * qg - dm) for qg in range(4)]
    kmax = [min(15, 4 * qg + 3) for qg in range(4)]
    return spans, chunks, kmin, kmax

SCHED = [head_sched(h) for h in range(H)]
NCHUNK = [sum(len(c) for c in SCHED[h][1]) for h in range(H)]
CHUNKS_BEFORE = [0] * (H + 1)
for h in range(H):
    CHUNKS_BEFORE[h + 1] = CHUNKS_BEFORE[h] + NCHUNK[h]


class Sem:
    def __init__(self, nc, name):
        self.h = nc.alloc_semaphore(name=name)
        self.n = 0
    def bump(self, k=1):
        self.n += k
        return self.n


class Plan:
    ENGS = ("tensor", "vector", "scalar", "sync", "gpsimd")
    def __init__(self):
        self.ops = {e: [] for e in self.ENGS}
    def op(self, eng, fn, waits=(), incs=()):
        ws = [(s.h, int(v)) for (s, v) in waits if v > 0]
        ks = []
        for (s, k) in incs:
            s.bump(k)
            ks.append((s.h, k))
        self.ops[eng].append((fn, ws, ks))
    def emit(self, nc):
        with nc.allow_low_precision(reason="bf16 matmul operands by design; fp32 accumulation in psum"), nc.Block() as block:
            for eng in self.ENGS:
                ops = self.ops[eng]
                if not ops:
                    continue
                def run(engine, _ops=ops):
                    for fn, ws, ks in _ops:
                        for h, v in ws:
                            engine.wait_ge(h, v)
                        inst = fn(engine)
                        if ks:
                            assert inst is not None
                            inst.then_inc(*ks[0])
                            for h, k in ks[1:]:
                                engine.nop().then_inc(h, k)
                getattr(block, eng)(run)


def build_nc(dbg=False, nheads=H, dumps=()):
    nc = bass.Bass("TRN2", target_bir_lowering=False, detect_race_conditions=False)

    d_x   = nc.dram_tensor("x",    [S, D],   F32, kind="ExternalInput").ap()
    d_wq  = nc.dram_tensor("wq",   [D, D],   F32, kind="ExternalInput").ap()
    d_wk  = nc.dram_tensor("wk",   [D, D],   F32, kind="ExternalInput").ap()
    d_wv  = nc.dram_tensor("wv",   [D, D],   F32, kind="ExternalInput").ap()
    d_wo  = nc.dram_tensor("wo",   [D, D],   F32, kind="ExternalInput").ap()
    d_w1t = nc.dram_tensor("w1t",  [D, FH],  F32, kind="ExternalInput").ap()
    d_w1tb = nc.dram_tensor("w1t_bf16_stage", [128, 8 * 4096], BF16, kind="Internal").ap()
    d_w2t = nc.dram_tensor("w2t",  [FH, D],  F32, kind="ExternalInput").ap()
    d_bqB = nc.dram_tensor("bqB",  [128, 8], F32, kind="ExternalInput").ap()
    d_bkB = nc.dram_tensor("bkB",  [128, 8], F32, kind="ExternalInput").ap()
    d_b1B = nc.dram_tensor("b1B",  [128, 32], F32, kind="ExternalInput").ap()
    d_bv  = nc.dram_tensor("bv",   [1, D],   F32, kind="ExternalInput").ap()
    d_bo  = nc.dram_tensor("bo",   [1, D],   F32, kind="ExternalInput").ap()
    d_b2  = nc.dram_tensor("b2",   [1, D],   F32, kind="ExternalInput").ap()
    d_scl = nc.dram_tensor("scl",  [1, 8],   F32, kind="ExternalInput").ap()
    d_ali = nc.dram_tensor("alibi",[128, 256], F32, kind="ExternalInput").ap()
    d_qps = nc.dram_tensor("qpos", [128, 16], F32, kind="ExternalInput").ap()
    d_idb = nc.dram_tensor("identb", [128, 128], BF16, kind="ExternalInput").ap()
    d_mmk = nc.dram_tensor("minmask", [128, 128], BF16, kind="ExternalInput").ap()
    d_amk = nc.dram_tensor("amask", [128, 128], F32, kind="ExternalInput").ap()
    d_one = nc.dram_tensor("onescol", [1, 128], BF16, kind="ExternalInput").ap()
    d_sel = nc.dram_tensor("selmat", [16, 2048], BF16, kind="ExternalInput").ap()
    d_out = nc.dram_tensor("out",  [S, D],   F32, kind="ExternalOutput").ap()
    kindd = "ExternalOutput" if dbg else "Internal"
    d_xnd = nc.dram_tensor("xn_scratch", [128, 16384], BF16, kind=kindd).ap()
    d_std = nc.dram_tensor("stage_scratch", [128, 16384], BF16, kind=kindd).ap()

    base = None
    arena = nc.alloc_sbuf_tensor("arena", [128, 210688], mybir.dt.uint8)
    base = nc.lookup_mloc(arena).addr

    def at(name, off, shape, dt):
        return nc.alloc_sbuf_tensor_at(name, shape, dt, offset=base + off).ap()

    xn_FM  = at("xn_FM",  0,     [128, 16384], BF16)
    qaug   = [at(f"qaug{i}", 0 + 4096 * i, [65, 2048], BF16) for i in range(2)]
    kaug   = [at(f"kaug{i}", 8192 + 4096 * i, [65, 2048], BF16) for i in range(2)]
    P_T    = ([at(f"pt{i}", 16384 + 4096 * i, [128, 2048], BF16) for i in range(2)]
              + [at(f"pt{i}", 177920 + 4096 * (i - 2), [128, 2048], BF16) for i in range(2, 4)])
    accS   = [at(f"accs{i}", 24576 + 1024 * i, [64, 512], BF16) for i in range(2)]
    vT_sb  = at("vt_sb", 188672, [4, 512], BF16)
    m_all  = at("m_all", 78784, [128, 16], F32)
    vbtmp  = at("vbtmp", 78848, [128, 16], F32)
    vb     = at("vb",    78912, [128, 16], BF16)
    hgel   = at("hgel",  0,     [128, 8192], BF16)
    xnring = [at(f"xnr{i}", 16384 + 4096 * i, [128, 2048], BF16) for i in range(2)]
    sgring = [at(f"sgr{i}", 24576 + 2048 * i, [128, 1024], BF16) for i in range(2)]
    xring  = [at(f"xr{i}", 32768 + 8192 * i, [128, 2048], F32) for i in range(3)]
    sqt    = [at(f"sq{i}", 57344 + 2048 * i, [128, 1024], BF16) for i in range(2)]
    xntm   = [at(f"xtm{i}", 61440 + 2048 * i, [128, 1024], BF16) for i in range(2)]
    attnFM = at("attnFM", 32768, [128, 16384], BF16)
    xres   = [at(f"xres{i}", 32768 + 4096 * i, [128, 1024], F32) for i in range(2)]
    outsb  = [at(f"osb{i}", 40960 + 4096 * i, [128, 1024], F32) for i in range(2)]

    alibi  = at("alibi_sb", 65536, [128, 256], F32)
    qpos   = at("qpos_sb", 66560, [128, 16], F32)
    identB = at("identb_sb", 66624, [128, 128], BF16)
    minmask= at("minmask_sb", 66880, [128, 128], BF16)
    onescol= at("onescol_sb", 67136, [1, 128], BF16)
    lora   = at("loratoks", 67392, [16, 1024], BF16)
    bqB    = at("bqB_sb", 69440, [128, 8], F32)
    bkB    = at("bkB_sb", 69472, [128, 8], F32)
    b1B    = at("b1B_sb", 69504, [128, 32], F32)
    sclsb  = at("scl_sb", 69632, [1, 8], F32)
    bvbf   = at("bvbf", 69664, [1, 1024], BF16)
    bb2    = at("bb2", 71712, [2, 1024], BF16)
    ones2  = at("ones2", 73760, [2, 128], BF16)
    oddcT  = at("oddcT", 75808, [128, 128], F32)
    lnst   = at("lnst", 76320, [128, 96], F32)
    scalebf = at("scalebf", 76672, [1, 8], BF16)
    scale_rep = at("scale_rep", 76704, [128, 8], F32)
    evr1 = [at(f"evr1_{i}", 76736 + 1024 * i, [128, 512], BF16) for i in range(2)]
    evr2 = [at(f"evr2_{i}", 76736 + 512 * i, [128, 256], BF16) for i in range(2)]
    amask  = at("amask_sb", 186112, [128, 128], F32)
    lnrow  = at("lnrow", 186624, [1, 512], F32)
    recB   = at("recB", 27904, [1, 2048], BF16)
    zcol   = at("zcol", 32000, [128, 72], BF16)
    selmat = at("selmat", 28672, [16, 2048], BF16)
    Q_FM   = at("Q_FM", 79104, [128, 16384], BF16)
    K_FM   = at("K_FM", 111872, [128, 16384], BF16)
    w1t_sb = at("w1t_sb", 79104, [128, 32768], BF16)
    V_pad  = at("V_pad", 144640, [128, 16640], BF16)
    wbuf   = [at(f"wbuf{i}", 177920 + 16384 * i, [128, 8192], BF16) for i in range(2)]
    w2t_sb = at("w2t_sb", 144640, [128, 32768], BF16)
    dstage = at("dstage", 210688, [128, 1024], BF16)

    sem_names = ["sx0", "sx0h", "sx1", "sx2", "smisc", "swq", "swk", "swv", "swo", "sw1", "sw2", "slt",
                 "saugq0", "saugq1", "saugk0", "saugk1", "saugv0", "saugv1",
                 "sxnr0", "sxnr1", "ssgr0", "ssgr1", "sxre0", "sxre1", "sout0", "sout1", "sw2b", "samk", "sdst0", "sdst1",
                 "sxnsp", "sxnsp2", "sxnsp3", "sxnsp4", "sw1s", "saug", "sxnr", "ssgr", "sdst", "sxre", "sout",
                 "aln_v", "aln_s", "aln_sq", "asum", "axn", "atp", "axnE",
                 "bmm", "bev",
                 "cdg", "cdgE", "cvb", "cvt", "cvtE", "csc", "cexp", "cmsk",
                 "cavk", "cavq", "crec", "crep", "cacc", "catt", "cmax",
                 "dmm", "dev",
                 "d2oc", "d2ocE", "d2sc", "d2er", "d2erE", "d2f", "d2fw2", "d2ev", "d2evw2",
                 "ep1", "egel", "ep2", "epf", "esd", "ssel", "sbias", "cpm", "vch"]
    SM = {n: Sem(nc, n) for n in sem_names}
    g = lambda n: SM[n]

    A = Plan()
    A.op("sync", lambda e: e.dma_start(
        xring[0][:, 0:1024], d_x[0:128, :]), incs=[(g("sx0h"), 16)])
    A.op("sync", lambda e: e.dma_start(
        xring[0][:, 1024:2048], d_x[128:256, :]), incs=[(g("sx0"), 16)])
    A.op("sync", lambda e: e.dma_start(
        xring[1][:, :].rearrange("p (j d) -> p j d", j=2),
        d_x[256:512, :].rearrange("(j p) d -> p j d", p=128)),
        incs=[(g("sx1"), 16)])
    for (dst, srcap) in [(alibi, d_ali), (qpos, d_qps), (identB, d_idb), (minmask, d_mmk),
                         (onescol, d_one), (bqB, d_bqB), (bkB, d_bkB), (b1B, d_b1B),
                         (sclsb, d_scl)]:
        A.op("sync", lambda e, a=dst, b=srcap: e.dma_start(a[:, :], b[:, :]),
             incs=[(g("smisc"), 16)])
    for (dst, srcap) in [(bvbf, d_bv), (bb2[0:1, :], d_bo), (bb2[1:2, :], d_b2)]:
        A.op("gpsimd", lambda e, a=dst, b=srcap: e.dma_start(a[:, :], b[:, :]),
             incs=[(g("sbias"), 16)])
    A.op("gpsimd", lambda e: e.dma_start(
        wbuf[0][:, :].rearrange("p (c o) -> p c o", c=8),
        d_wq.rearrange("(c p) o -> p c o", p=128)), incs=[(g("swq"), 16)])
    A.op("gpsimd", lambda e: e.dma_start(
        wbuf[1][:, :].rearrange("p (c o) -> p c o", c=8),
        d_wk.rearrange("(c p) o -> p c o", p=128)), incs=[(g("swk"), 16)])
    for c in range(8):
        w1cw = [(g("axn"), 8)] if c == 0 else []
        A.op("gpsimd", lambda e, c=c: e.dma_start(
            d_w1tb[:, c * 4096:(c + 1) * 4096], d_w1t[c * 128:(c + 1) * 128, :]),
            waits=w1cw, incs=[(g("sw1s"), 16)])
    vones = V_pad[:, :].rearrange("p (t h e) -> p t h e", t=16, h=16)[:, :, :, 64:65]
    A.op("vector", lambda e: e.memset(vones, 1.0))
    A.op("vector", lambda e: e.memset(ones2[:, :], 1.0))
    for pr in range(2, 8):
        waits = [(g("axn"), 2 * pr - 4)] if pr >= 3 else []
        A.op("sync", lambda e, pr=pr: e.dma_start(
            xring[pr % 3][:, :].rearrange("p (j d) -> p j d", j=2),
            d_x[pr * 256:(pr + 1) * 256, :].rearrange("(j p) d -> p j d", p=128)),
            waits=waits, incs=[(g("sx%d" % (pr % 3)), 16)])
        if pr == 3:
            A.op("sync", lambda e: e.dma_start(lora[:, :], xntm[0][1:17, :]),
                 waits=[(g("axn"), 1)], incs=[(g("slt"), 16)])
    use = 0

    def xwait(qt):
        if qt == 0:
            return (g("sx0h"), 16)
        return (g("sx%d" % ((qt // 2) % 3)), 16 * ((qt // 2) // 3 + 1))

    def emit_square(q2):
        xr2 = xring[(q2 // 2) % 3][:, (q2 % 2) * 1024:(q2 % 2) * 1024 + 1024]
        waits = [xwait(q2)]
        if q2 >= 2:
            waits.append((g("asum"), q2 - 1))
        A.op("scalar", lambda e, xr2=xr2, q2=q2: e.activation(sqt[q2 % 2][:, :], xr2, AF.Square),
             waits=waits, incs=[(g("aln_sq"), 1)])

    def emit_tile_sums(qt):
        emit_square(qt)
        xr = xring[(qt // 2) % 3][:, (qt % 2) * 1024:(qt % 2) * 1024 + 1024]
        A.op("vector", lambda e, xr=xr, qt=qt: e.reduce_sum(lnst[:, qt:qt + 1], xr, AX.X),
             waits=[xwait(qt)])
        A.op("vector", lambda e, qt=qt: e.reduce_sum(lnst[:, 16 + qt:17 + qt], sqt[qt % 2][:, :], AX.X),
             waits=[(g("aln_sq"), qt + 1)], incs=[(g("asum"), 1)])

    def emit_ln_a(gq):
        if gq < 2:
            for qt in range(4 * gq, 4 * gq + 4):
                emit_tile_sums(qt)
        base = 32 + 28 * (gq % 2)
        mean4 = lnst[:, base:base + 4]; t4 = lnst[:, base + 4:base + 8]
        msq4 = lnst[:, base + 8:base + 12]; vpe4 = lnst[:, base + 12:base + 16]
        sq4 = lnst[:, base + 16:base + 20]; r4 = lnst[:, base + 20:base + 24]
        negmr4 = lnst[:, base + 24:base + 28]
        A.op("vector", lambda e, gq=gq, o=mean4: e.tensor_scalar(o, lnst[:, 4 * gq:4 * gq + 4], 1.0 / 1024, None, OP.mult),
             waits=[(g("vch"), g("vch").n), (g("axn"), 4 * gq - 4)], incs=[(g("vch"), 1)])
        A.op("vector", lambda e, gq=gq, o=t4: e.tensor_scalar(o, lnst[:, 16 + 4 * gq:20 + 4 * gq], 1.0 / 1024, EPS, OP.mult, OP.add),
             waits=[(g("vch"), g("vch").n)], incs=[(g("vch"), 1)])
        A.op("vector", lambda e, i=mean4, o=msq4: e.tensor_tensor(o, i, i, OP.mult),
             waits=[(g("vch"), g("vch").n)], incs=[(g("vch"), 1)])
        A.op("vector", lambda e, a=t4, b=msq4, o=vpe4: e.tensor_tensor(o, a, b, OP.subtract),
             waits=[(g("vch"), g("vch").n)], incs=[(g("aln_v"), 1), (g("vch"), 1)])
        A.op("scalar", lambda e, i=vpe4, o=sq4: e.activation(o, i, AF.Sqrt),
             waits=[(g("aln_v"), 2 * gq + 1)], incs=[(g("aln_s"), 1)])
        A.op("vector", lambda e, i=sq4, o=r4: e.reciprocal(o, i),
             waits=[(g("aln_s"), gq + 1)], incs=[(g("vch"), 1)])
        A.op("vector", lambda e, i=mean4, r=r4, o=negmr4: e.scalar_tensor_tensor(
            o, i, -1.0, r, OP.mult, OP.mult),
            waits=[(g("vch"), g("vch").n)], incs=[(g("aln_v"), 1), (g("vch"), 1)])
        def emit_act(qt):
            xr = xring[(qt // 2) % 3][:, (qt % 2) * 1024:(qt % 2) * 1024 + 1024]
            j4c = qt - 4 * gq
            waits = [(g("aln_v"), 2 * gq + 2), (g("atp"), 2 * (qt - 1))]
            if qt == 2:
                waits.append((g("slt"), 16))
            A.op("scalar", lambda e, xr=xr, qt=qt, r=r4[:, j4c:j4c + 1], b=negmr4[:, j4c:j4c + 1]: e.activation(
                xntm[qt % 2][:, :], xr, AF.Identity, bias=b, scale=r),
                waits=waits, incs=[(g("axn"), 1)])
        for qt in range(4 * gq, 4 * gq + 2):
            emit_act(qt)
        acts_b[gq] = emit_act
        return

    acts_b = {}

    def emit_ln_b(gq):
        for qt in range(4 * gq + 2, 4 * gq + 4):
            acts_b[gq](qt)
        for qt in range(4 * gq, 4 * gq + 4):
            for b in range(2):
                for j4 in range(4):
                    j = b * 4 + j4
                    waits = []
                    if j == 0:
                        waits = [(g("axn"), qt + 1), (g("axnE"), 2 * (qt - 1))]
                        if qt == 0:
                            waits.append((g("smisc"), 144))
                    incs = [(g("atp"), 1)] if j4 == 3 else []
                    A.op("tensor", lambda e, qt=qt, b=b, j4=j4, j=j: e.transpose(
                        PA[(qt % 2) * 2 + b][:, j4 * 128:(j4 + 1) * 128],
                        xntm[qt % 2][:, j * 128:(j + 1) * 128], identB[:, :]),
                        waits=waits, incs=incs)
                dst = xn_FM[:, :].rearrange("p (c q) -> p c q", c=8)[:, b * 4:(b + 1) * 4, qt * 128:(qt + 1) * 128]
                src_fn = lambda e, qt=qt, b=b, dst=dst: e.tensor_copy(
                    dst, PA[(qt % 2) * 2 + b][:, :].rearrange("p (c q) -> p c q", c=4))
                A.op("vector", src_fn, waits=[(g("atp"), 2 * qt + b + 1)], incs=[(g("axnE"), 1)])

    def emit_qk(gq):
        nonlocal use
        for (wsem, wsb, dstT, bias) in [("swq", 0, Q_FM, bqB), ("swk", 1, K_FM, bkB)]:
            for c in range(8):
                for k in range(8):
                    waits = []
                    if k == 0:
                        waits = [(g("bev"), use - 3)]
                        if c == 0:
                            waits.append((g("axnE"), 8 * gq + 8))
                            waits.append((g(wsem), 16))
                    incs = [(g("bmm"), 1)] if k == 7 else []
                    A.op("tensor", lambda e, u=use, w=wsb, c=c, gq=gq, k=k: e.matmul(
                        PB[u % 4][:, :],
                        wbuf[w][:, k * 1024 + c * 128: k * 1024 + (c + 1) * 128],
                        xn_FM[:, k * 2048 + gq * 512: k * 2048 + (gq + 1) * 512],
                        start=(k == 0), stop=(k == 7)),
                        waits=waits, incs=incs)
                bw = [(g("bmm"), use + 1)]
                if use == 0:
                    bw.append((g("smisc"), 144))
                A.op("scalar", lambda e, u=use, dstT=dstT, bias=bias, c=c, gq=gq: e.activation(
                    dstT[:, c * 2048 + gq * 512: c * 2048 + (gq + 1) * 512],
                    PB[u % 4][:, :], AF.Identity, bias=bias[:, c:c + 1]),
                    waits=bw, incs=[(g("bev"), 1)])
                use += 1
                if wsb == 0 and c % 2 == 1 and gq + 2 < 4:
                    emit_tile_sums(4 * (gq + 2) + (c - 1) // 2)

    emit_ln_a(0)
    emit_ln_b(0)
    emit_ln_a(1)
    for gq in range(4):
        emit_qk(gq)
        if gq + 1 < 4:
            emit_ln_b(gq + 1)
        if gq + 2 < 4:
            emit_ln_a(gq + 2)
    A.op("gpsimd", lambda e: e.dma_start(
        wbuf[0][:, :].rearrange("p (c o) -> p c o", c=8),
        d_wv.rearrange("(c p) o -> p c o", p=128)),
        waits=[(g("bmm"), 56)], incs=[(g("swv"), 16)])
    A.op("gpsimd", lambda e: e.dma_start(
        wbuf[1][:, :].rearrange("p (c o) -> p c o", c=8),
        d_wo.rearrange("(c p) o -> p c o", p=128)),
        waits=[(g("bmm"), 64)], incs=[(g("swo"), 16)])
    for qt in range(NQT):
        for vh in range(2):
            waits = [(g("bev"), use - 3)]
            if qt == 0 and vh == 0:
                waits.append((g("swv"), 16))
                waits.append((g("sbias"), 48))
            A.op("tensor", lambda e, u=use, vh=vh: e.matmul(
                PB[u % 4][:, :], onescol[0:1, :], bvbf[0:1, vh * 512:(vh + 1) * 512],
                start=True, stop=False), waits=waits)
            for k in range(8):
                incs = [(g("bmm"), 1)] if k == 7 else []
                A.op("tensor", lambda e, u=use, qt=qt, vh=vh, k=k: e.matmul(
                    PB[u % 4][:, :],
                    xn_FM[:, k * 2048 + qt * 128: k * 2048 + (qt + 1) * 128],
                    wbuf[0][:, k * 1024 + vh * 512: k * 1024 + (vh + 1) * 512],
                    start=False, stop=(k == 7)), incs=incs)
            dst = V_pad[:, qt * 1040 + vh * 8 * 65: qt * 1040 + (vh + 1) * 8 * 65].rearrange(
                "p (h s) -> p h s", h=8)[:, :, 0:64]
            A.op("scalar", lambda e, u=use, dst=dst: e.activation(
                dst, PB[u % 4][:, :].rearrange("p (h s) -> p h s", h=8), AF.Identity),
                waits=[(g("bmm"), use + 1)], incs=[(g("bev"), 1)])
            use += 1

    A.op("sync", lambda e: e.dma_start(d_xnd[:, 0:4096], xn_FM[:, 0:4096]),
         waits=[(g("axnE"), 32)], incs=[(g("sxnsp"), 16)])
    A.op("scalar", lambda e: e.dma_start(d_xnd[:, 4096:8192], xn_FM[:, 4096:8192]),
         waits=[(g("axnE"), 32)], incs=[(g("sxnsp2"), 16)])
    A.op("gpsimd", lambda e: e.dma_start(d_xnd[:, 12288:16384], xn_FM[:, 12288:16384]),
         waits=[(g("axnE"), 32)], incs=[(g("sxnsp4"), 16)])
    A.op("gpsimd", lambda e: e.dma_start(d_xnd[:, 8192:12288], xn_FM[:, 8192:12288]),
         incs=[(g("sxnsp3"), 16)])
    A.op("vector", lambda e: e.memset(kaug[0][64:65, :], 1.0), waits=[(g("sxnsp2"), 16)])
    A.op("vector", lambda e: e.memset(kaug[1][64:65, :], 1.0))
    A.op("vector", lambda e: e.memset(zcol[:, :], 0.0), waits=[(g("sxnsp4"), 16)])

    C = Plan()
    C.op("gpsimd", lambda e: e.dma_start(amask[:, :], d_amk[:, :]), incs=[(g("samk"), 16)])

    slot_consumer = []
    csc_after = []
    nexp = [0]
    nmax = [0]

    def slot_producer_waits(s):
        w = []
        if s >= 4:
            w.append((g(slot_consumer[s - 4][0]), slot_consumer[s - 4][1]))
        return w

    exp_after = {}
    csc_after_head = {}

    def emit_diag_slots(hh, s0):
        c2, r0 = hh >> 1, 64 * (hh & 1)
        for half in range(4):
            s = s0 + half
            w0 = slot_producer_waits(s)
            for j in range(4):
                qt = 4 * half + j
                waits = list(w0) if j == 0 else []
                incs = [(g("csc"), 1)] if j == 3 else []
                C.op("tensor", lambda e, c2=c2, r0=r0, qt=qt, j=j, s=s: e.matmul(
                    PSC[s % 4][:, j * 128:(j + 1) * 128],
                    Q_FM[r0:r0 + 64, c2 * 2048 + qt * 128: c2 * 2048 + (qt + 1) * 128],
                    K_FM[r0:r0 + 64, c2 * 2048 + qt * 128: c2 * 2048 + (qt + 1) * 128],
                    start=True, stop=True), waits=waits, incs=incs)
            csc_after.append(g("csc").n)
            C.op("vector", lambda e, half=half, s=s: e.reduce_max(
                m_all[:, half * 4:(half + 1) * 4],
                PSC[s % 4][:, :].rearrange("p (a q) -> p a q", a=4), AX.X),
                waits=[(g("csc"), csc_after[s])], incs=[(g("cmax"), 1)])
            nmax[0] += 1
            slot_consumer.append(("cmax", nmax[0]))

    def emit_vt_slot(hh, s):
        w0 = slot_producer_waits(s)
        for gg in range(4):
            waits = list(w0) + [(g("cvb"), hh + 1)] if gg == 0 else []
            incs = [(g("csc"), 1), (g("cvt"), 1)] if gg == 3 else []
            C.op("tensor", lambda e, gg=gg, s=s: e.matmul(
                PSC[s % 4][0:4, gg * 128:(gg + 1) * 128],
                vb[:, :].rearrange("p (m q) -> p q m", q=4)[:, gg, :], identB[:, :],
                start=True, stop=True), waits=waits, incs=incs)
        csc_after.append(g("csc").n)
        waits = [(g("csc"), csc_after[s])]
        if hh >= 1:
            waits.append((g("saugv%d" % ((hh - 1) % 2)), 16 * ((hh - 1) // 2 + 1)))
        C.op("vector", lambda e, s=s: e.tensor_copy(vT_sb[:, :], PSC[s % 4][0:4, 0:512]),
             waits=waits, incs=[(g("cmax"), 1), (g("cvtE"), 1)])
        nmax[0] += 1
        slot_consumer.append(("cmax", nmax[0]))

    def emit_av(oi, h, kb):
        spans, chunks, kmin, kmax = SCHED[h]
        qend = spans[kb][1]
        pt = P_T[(16 * oi + kb) % 4]
        qgs = [qg for qg in range(kb // 4, 4) if qg * 512 < qend]
        last_qg = qgs[-1]
        for qi, qg in enumerate(qgs):
            q0 = max(kb * 128, qg * 512)
            q1 = min(qend, (qg + 1) * 512)
            waits = []
            if qi == 0:
                waits.append((g("cexp"), exp_after[(h, kb)]))
                waits.append((g("cmsk"), oi * 16 + kb + 1))
            if kb == kmin[qg]:
                zw = [(g("cacc"), 4 * (oi - 1) + qg + 1), (g("crec"), 4 * (oi - 1) + qg + 1),
                      (g("catt"), 4 * (oi - 1) + qg + 1)]
                C.op("tensor", lambda e, qg=qg: e.matmul(
                    PAV[qg][0:65, 0:512], zcol[:, 0:65], V_pad[:, 0:512],
                    start=True, stop=False), waits=zw)
            stop = (kb == kmax[qg])
            incs = []
            if stop:
                incs.append((g("cavq"), 1))
            if qg == last_qg:
                incs.append((g("cavk"), 1))
            C.op("tensor", lambda e, h=h, kb=kb, qg=qg, q0=q0, q1=q1, stop=stop, pt=pt: e.matmul(
                PAV[qg][0:65, q0 - qg * 512: q1 - qg * 512],
                V_pad[:, kb * 1040 + h * 65: kb * 1040 + h * 65 + 65],
                pt[:, q0:q1], start=False, stop=stop),
                waits=waits, incs=incs)

    def emit_vb(oi, hh):
        vbw = [(g("cvt"), oi), (g("vch"), 0)]
        if oi == 0:
            vbw.append((g("smisc"), 144))
        C.op("vector", lambda e, hh=hh: e.tensor_scalar(vbtmp[:, :], qpos[:, :], -8.0 * SLOPES[hh], None, OP.mult),
             waits=vbw, incs=[(g("vch"), 1)])
        C.op("vector", lambda e: e.tensor_tensor(vb[:, :], vbtmp[:, :], m_all[:, :], OP.subtract),
             waits=[(g("vch"), g("vch").n), (g("cmax"), nmax[0])], incs=[(g("cvb"), 1)])

    pending_rn = []
    pending_pre = []

    def emit_renorm_pre(oi, qg):
        C.op("scalar", lambda e, qg=qg: e.activation(lnrow[0:1, :], PAV[qg][64:65, :], AF.Ln),
             waits=[(g("cavq"), 4 * oi + qg + 1), (g("crep"), 4 * (oi - 1) + qg + 1)])
        C.op("scalar", lambda e, qg=qg: e.activation(
            recB[0:1, qg * 512:(qg + 1) * 512], lnrow[0:1, :], AF.Exp, scale=-1.0),
            incs=[(g("crec"), 1)])

    def emit_renorm_copy(oi, qg):
        C.op("vector", lambda e, qg=qg: e.tensor_copy(
            accS[qg % 2][:, :], PAV[qg][0:64, :]),
            waits=[(g("cavq"), 4 * oi + qg + 1), (g("catt"), 4 * oi + qg - 1)],
            incs=[(g("cacc"), 1)])

    def emit_renorm_qg(oi, c2, r0, qg):
        if qg >= 2:
            emit_renorm_copy(oi, qg)
        C.op("tensor", lambda e, qg=qg: e.matmul(
            PAV[qg][0:64, 0:512], onescol[0:1, 0:64],
            recB[0:1, qg * 512:(qg + 1) * 512],
            start=True, stop=True),
            waits=[(g("crec"), 4 * oi + qg + 1), (g("cacc"), 4 * oi + qg + 1)],
            incs=[(g("crep"), 1)])
        C.op("vector", lambda e, c2=c2, r0=r0, qg=qg: e.tensor_tensor(
            attnFM[r0:r0 + 64, c2 * 2048 + qg * 512: c2 * 2048 + (qg + 1) * 512],
            accS[qg % 2][:, :], PAV[qg][0:64, 0:512], OP.mult),
            waits=[(g("crep"), 4 * oi + qg + 1), (g("cacc"), 4 * oi + qg + 1)],
            incs=[(g("catt"), 1)])

    HORD = list(range(nheads))
    pos_of = {hh: oo for oo, hh in enumerate(HORD)}

    for oi, h in enumerate(HORD):
        c2, r0 = h >> 1, 64 * (h & 1)
        spans, chunks, kmin, kmax = SCHED[h]
        if oi == 0:
            emit_diag_slots(0, 0)
            emit_vb(0, 0)
            emit_vt_slot(0, 4)
        waits = [(g("sxnsp"), 16)]
        if oi >= 2:
            waits.append((g("csc"), csc_after_head[oi - 2]))
        C.op("sync", lambda e, oi=oi, c2=c2, r0=r0: e.dma_start(
            qaug[oi % 2][0:64, :], Q_FM[r0:r0 + 64, c2 * 2048:(c2 + 1) * 2048]),
            waits=waits, incs=[(g("saugq%d" % (oi % 2)), 16)])
        kgw = [(g("sxnsp2"), 16)] if oi == 0 else []
        C.op("sync", lambda e, oi=oi, c2=c2, r0=r0: e.dma_start(
            kaug[oi % 2][0:64, :], K_FM[r0:r0 + 64, c2 * 2048:(c2 + 1) * 2048]),
            waits=kgw, incs=[(g("saugk%d" % (oi % 2)), 16)])
        C.op("sync", lambda e, oi=oi: e.dma_start(
            qaug[oi % 2][64:65, :].rearrange("p (m r) -> p m r", m=4),
            vT_sb[:, :]),
            waits=[(g("cvtE"), oi + 1)], incs=[(g("saugv%d" % (oi % 2)), 16)])
        pc = [len(chunks[kb_]) for kb_ in range(NKB)]
        tot_pc = sum(pc)
        kb_d, kb_e = NKB - 4, NKB - 3
        cum = 0
        for kb_ in range(NKB):
            cum += pc[kb_]
            if tot_pc - cum <= 14 and kb_d == NKB - 4:
                kb_d = max(1, min(kb_, NKB - 4))
            if tot_pc - cum <= 8:
                kb_e = max(kb_d + 1, min(kb_, NKB - 3))
                break
        for kb in range(NKB):
            pt = P_T[(16 * oi + kb) % 4]
            q0span, qend = spans[kb]
            pieces = []
            p0 = q0span
            while p0 < qend:
                p1 = min(qend, (p0 // 512 + 1) * 512)
                pieces.append((p0, p1))
                p0 = p1
            for pi, (p0, p1) in enumerate(pieces):
                s = len(slot_consumer)
                w0 = slot_producer_waits(s)
                if kb == 0 and pi == 0:
                    w0.append((g("saugq%d" % (oi % 2)), 16 * (oi // 2 + 1)))
                    w0.append((g("saugk%d" % (oi % 2)), 16 * (oi // 2 + 1)))
                    w0.append((g("saugv%d" % (oi % 2)), 16 * (oi // 2 + 1)))
                subs = []
                c0s = p0
                while c0s < p1:
                    c1s = min(p1, c0s + 512)
                    subs.append((c0s, c1s))
                    c0s = c1s
                for si, (c0s, c1s) in enumerate(subs):
                    waits = list(w0) if si == 0 else []
                    incs = [(g("csc"), 1)] if si == len(subs) - 1 else []
                    C.op("tensor", lambda e, oi=oi, kb=kb, c0s=c0s, c1s=c1s, p0=p0, s=s: e.matmul(
                        PSC[s % 4][:, c0s - p0: c1s - p0],
                        kaug[oi % 2][:, kb * 128:(kb + 1) * 128],
                        qaug[oi % 2][:, c0s:c1s], start=True, stop=True),
                        waits=waits, incs=incs)
                csc_after.append(g("csc").n)
                if h == 0 and pi == 0:
                    C.op("vector", lambda e, s=s, p0=p0, kb=kb: e.tensor_tensor(
                        PSC[s % 4][:, kb * 128 - p0: (kb + 1) * 128 - p0],
                        PSC[s % 4][:, kb * 128 - p0: (kb + 1) * 128 - p0], amask[:, :], OP.add),
                        waits=[(g("csc"), csc_after[s]), (g("samk"), 16)], incs=[(g("cpm"), 1)])
                waits = [(g("csc"), csc_after[s]), (g("cavk"), oi * 16 + kb - 3),
                         (g("cmsk"), oi * 16 + kb - 3)]
                if h == 0 and pi == 0:
                    waits.append((g("cpm"), kb + 1))
                if h == 0 and kb == 0 and pi == 0:
                    waits.append((g("sxnsp3"), 16))
                    waits.append((g("sxnsp4"), 16))
                    waits.append((g("smisc"), 144))
                C.op("scalar", lambda e, h=h, kb=kb, p0=p0, p1=p1, s=s, pt=pt: e.activation(
                    pt[:, p0:p1], PSC[s % 4][:, 0:p1 - p0], AF.Exp,
                    bias=alibi[:, h * 16 + kb: h * 16 + kb + 1], scale=0.125),
                    waits=waits, incs=[(g("cexp"), 1)])
                nexp[0] += 1
                slot_consumer.append(("cexp", nexp[0]))
                if pi == 0:
                    C.op("vector", lambda e, kb=kb, pt=pt: e.tensor_tensor(
                        pt[:, kb * 128:(kb + 1) * 128],
                        pt[:, kb * 128:(kb + 1) * 128], minmask[:, :], OP.min),
                        waits=[(g("cexp"), nexp[0])], incs=[(g("cmsk"), 1)])
                if pending_pre:
                    emit_renorm_pre(*pending_pre.pop(0))
                if pending_rn:
                    emit_renorm_qg(*pending_rn.pop(0))
            exp_after[(h, kb)] = nexp[0]
            if h == 0 and kb == 0:
                pass
            if kb == kb_d and oi + 1 < nheads:
                emit_diag_slots(HORD[oi + 1], len(slot_consumer))
                emit_vb(oi + 1, HORD[oi + 1])
            if kb == kb_e and oi + 1 < nheads:
                emit_vt_slot(oi + 1, len(slot_consumer))
            if kb >= 2:
                while pending_pre:
                    emit_renorm_pre(*pending_pre.pop(0))
                while pending_rn:
                    emit_renorm_qg(*pending_rn.pop(0))
                emit_av(oi, h, kb - 2)
            if kb == NKB - 1:
                csc_after_head[oi] = g("csc").n
                emit_av(oi, h, kb - 1)
                emit_av(oi, h, kb)
        while pending_pre:
            emit_renorm_pre(*pending_pre.pop(0))
        emit_renorm_pre(oi, 0)
        for qg in range(2):
            emit_renorm_copy(oi, qg)
        if oi + 1 < nheads:
            if sum(len(c) for c in SCHED[HORD[oi + 1]][1]) >= 12:
                pending_pre.extend((oi, qg) for qg in range(1, 4))
            else:
                for qg in range(1, 4):
                    emit_renorm_pre(oi, qg)
            pending_rn.extend((oi, c2, r0, qg) for qg in range(4))
        else:
            for qg in range(1, 4):
                emit_renorm_pre(oi, qg)
            for qg in range(4):
                emit_renorm_qg(oi, c2, r0, qg)

    for i in range(4):
        gate = csc_after_head[max(pos_of[hh] for hh in range(4 * i, min(4 * i + 4, nheads)))]
        for half in range(2):
            c = i if half == 0 else i + 4
            C.op("gpsimd", lambda e, c=c: e.dma_start(
                w1t_sb[:, c * 4096:(c + 1) * 4096], d_w1tb[:, c * 4096:(c + 1) * 4096]),
                waits=[(g("csc"), gate), (g("sw1s"), 128)], incs=[(g("sw1"), 16)])

    Dp = Plan()
    _dbg_tensors = dict(qaug0=qaug[0], kaug0=kaug[0], m_all=m_all, vb=vb, vT_sb=vT_sb,
                        attnFM=attnFM, pt0=P_T[0], pt1=P_T[1], recB=recB,
                        V_pad=V_pad, Q_FM=Q_FM, K_FM=K_FM, accS0=accS[0], accS1=accS[1])
    if dumps:
        SM["sdump"] = Sem(nc, "sdump")
    for _dn in dumps:
        _t = _dbg_tensors[_dn]
        _dd = nc.dram_tensor("dump_" + _dn, list(_t.shape), _t.dtype, kind="ExternalOutput").ap()
        Dp.op("sync", lambda e, a=_dd, b=_t: e.dma_start(a[:, :], b[:, :]), incs=[(g("sdump"), 16)])
    if dumps:
        Dp.op("sync", lambda e: e.wait_ge(g("sdump").h, 16 * len(dumps)))

    for ci in range(8):
        w2w = [(g("dmm"), 32)] if ci >= 6 else []
        w2sem = "sw2b" if ci >= 6 else "sw2"
        Dp.op("gpsimd", lambda e, ci=ci: e.dma_start(
            w2t_sb[:, ci * 4096:(ci + 1) * 4096].rearrange("p (q o) -> p q o", q=4),
            d_w2t[ci * 512:(ci + 1) * 512, :].rearrange("(q p) o -> p q o", p=128)),
            waits=w2w, incs=[(g(w2sem), 16)])

    Dp.op("sync", lambda e: e.dma_start(selmat[:, :], d_sel[:, :]), incs=[(g("ssel"), 16)])
    Dp.op("sync", lambda e: e.dma_start(
        xnring[0][:, :].rearrange("p (c q) -> p c q", c=8),
        d_xnd[:, :].rearrange("p (c q) -> p c q", c=8)[:, :, 0:256]),
        waits=[(g("sxnsp2"), 16), (g("sxnsp3"), 16), (g("sxnsp4"), 16)], incs=[(g("sxnr0"), 16)])

    Dp.op("vector", lambda e: e.tensor_scalar(scalebf[0:1, 0:1], sclsb[0:1, 0:1], 1.0, None, OP.mult),
         waits=[(g("smisc"), 144)], incs=[(g("d2sc"), 1)])
    for c in range(8):
        Dp.op("tensor", lambda e, c=c: e.matmul(
            PE2[:, c * 16:(c + 1) * 16], lora[:, c * 128:(c + 1) * 128], identB[0:16, 0:16],
            start=True, stop=True))
    Dp.op("tensor", lambda e: e.matmul(
        PE2[:, 128:129], onescol[0:1, :], scalebf[0:1, 0:1], start=True, stop=True),
        waits=[(g("d2sc"), 1)], incs=[(g("d2oc"), 1)])
    Dp.op("vector", lambda e: e.tensor_copy(scale_rep[:, 0:1], PE2[:, 128:129]),
         waits=[(g("d2oc"), 1)], incs=[(g("vch"), 1)])
    Dp.op("vector", lambda e: e.tensor_scalar(oddcT[:, :], PE2[:, 0:128], scale_rep[:, 0:1], None, OP.mult),
         waits=[(g("vch"), g("vch").n)], incs=[(g("d2ocE"), 1), (g("vch"), 1)])

    w1_half = [0]
    def emit_w1_half():
        i = w1_half[0]
        if i >= 8:
            return
        j, hf = i // 2, i % 2
        waits = [(g("ssel"), 16), (g("d2ocE"), 1)]
        if i >= 2:
            waits.append((g("d2ev"), i - 1))
        Dp.op("tensor", lambda e, j=j, hf=hf, i=i: e.matmul(
            PE2[:, (i % 2) * 512:(i % 2) * 512 + 512],
            selmat[:, 2 * j * 128:(2 * j + 1) * 128], lora[:, hf * 512:(hf + 1) * 512],
            start=True, stop=True), waits=waits, incs=[(g("d2er"), 1)])
        ew = [(g("d2er"), i + 1)]
        if i >= 2:
            ew.append((g("d2erE"), i - 1))
        Dp.op("scalar", lambda e, i=i: e.activation(
            evr1[i % 2][:, :], PE2[:, (i % 2) * 512:(i % 2) * 512 + 512], AF.Identity),
            waits=ew, incs=[(g("d2ev"), 1)])
        for c in range(8):
            incs = []
            if c == 7:
                incs.append((g("d2erE"), 1))
                if hf == 1:
                    incs.append((g("d2f"), 1))
            w1w = [(g("sw1"), 128)]
            if c == 0:
                w1w.append((g("d2ev"), i + 1))
            Dp.op("vector", lambda e, j=j, hf=hf, c=c, i=i: e.scalar_tensor_tensor(
                w1t_sb[:, c * 4096 + j * 1024 + hf * 512: c * 4096 + j * 1024 + hf * 512 + 512],
                evr1[i % 2][:, :],
                oddcT[:, c * 16 + 1 + 2 * j: c * 16 + 2 + 2 * j],
                w1t_sb[:, c * 4096 + j * 1024 + hf * 512: c * 4096 + j * 1024 + hf * 512 + 512],
                OP.mult, OP.add),
                waits=w1w, incs=incs)
        w1_half[0] += 1

    duse = 0
    for qt in range(NQT):
        for oh in range(2):
            for c in range(8):
                waits = []
                if c == 0:
                    waits = [(g("dev"), duse - 3)]
                    if duse == 0:
                        waits.append((g("swo"), 16))
                incs = [(g("dmm"), 1)] if c == 7 else []
                Dp.op("tensor", lambda e, u=duse, qt=qt, oh=oh, c=c: e.matmul(
                    POUT[u % 4][:, :],
                    attnFM[:, c * 2048 + qt * 128: c * 2048 + (qt + 1) * 128],
                    wbuf[1][:, c * 1024 + oh * 512: c * 1024 + (oh + 1) * 512],
                    start=(c == 0), stop=(c == 7)), waits=waits, incs=incs)
            dw = [(g("dmm"), duse + 1)]
            if qt >= 1:
                dw.append((g("sdst%d" % ((qt - 1) % 2)), 32 * ((qt - 1) // 2) + 16 * (oh + 1)))
            Dp.op("scalar", lambda e, u=duse, oh=oh: e.activation(
                dstage[:, oh * 512:(oh + 1) * 512], POUT[u % 4][:, :], AF.Identity),
                waits=dw, incs=[(g("dev"), 1)])
            duse += 1
            Dp.op("sync", lambda e, qt=qt, oh=oh: e.dma_start(
                d_std[:, qt * 1024 + oh * 512: qt * 1024 + (oh + 1) * 512],
                dstage[:, oh * 512:(oh + 1) * 512]),
                waits=[(g("dev"), 2 * qt + oh + 1)], incs=[(g("sdst%d" % (qt % 2)), 16)])
        if qt % 2 == 1:
            emit_w1_half()
    while w1_half[0] < 8:
        emit_w1_half()

    w2_group = [0]
    def emit_w2_group():
        gg = w2_group[0]
        if gg >= 4:
            return
        for a in range(4):
            er = 8 + 2 * a
            k = gg * 4 + a
            for hf in range(2):
                waits = []
                if hf == 0:
                    waits = [(g("d2evw2"), k)] if k >= 1 else [(g("d2ev"), 8)]
                incs = [(g("d2er"), 1)] if hf == 1 else []
                Dp.op("tensor", lambda e, er=er, hf=hf, waits=waits: e.matmul(
                    PE2[:, hf * 512:(hf + 1) * 512],
                    selmat[:, er * 128:(er + 1) * 128], lora[:, hf * 512:(hf + 1) * 512],
                    start=True, stop=True), waits=waits, incs=incs)
            ew = [(g("d2er"), 9 + k), (g("d2erE"), 8 + max(k - 1, 0))]
            Dp.op("scalar", lambda e, gg=gg, k=k: e.activation(
                evr2[k % 2][:, :],
                PE2[:, 0:1024].rearrange("p (o e) -> p e o", e=4)[:, gg, :],
                AF.Identity), waits=ew, incs=[(g("d2evw2"), 1)])
            for c2 in range(8):
                ch = gg * 8 + c2
                incs = []
                if c2 == 7:
                    incs.append((g("d2erE"), 1))
                    incs.append((g("d2fw2"), 1))
                ws = [(g("sw2"), 96)] if gg < 3 else [(g("sw2b"), 32)]
                if c2 == 0:
                    ws.append((g("d2evw2"), k + 1))
                Dp.op("vector", lambda e, a=a, gg=gg, c2=c2, ch=ch, k=k: e.scalar_tensor_tensor(
                    w2t_sb[:, ch * 1024 + a * 256: ch * 1024 + (a + 1) * 256],
                    evr2[k % 2][:, :],
                    oddcT[:, c2 * 16 + 9 + 2 * a: c2 * 16 + 10 + 2 * a],
                    w2t_sb[:, ch * 1024 + a * 256: ch * 1024 + (a + 1) * 256],
                    OP.mult, OP.add),
                    waits=ws, incs=incs)
        w2_group[0] += 1

    def emit_ring_loads(tg):
        for qt2 in range(2):
            qt = 2 * tg + qt2
            waits = [(g("sdst%d" % (qt % 2)), 32 * (qt // 2 + 1))]
            if tg >= 1:
                waits.append((g("esd"), tg))
            Dp.op("sync", lambda e, qt=qt, qt2=qt2: e.dma_start(
                sgring[qt2][:, :], d_std[:, qt * 1024:(qt + 1) * 1024]),
                waits=waits, incs=[(g("ssgr%d" % qt2), 16)])
        if tg >= 1:
            waits = [(g("ep1"), 32 * (tg - 1))] if tg >= 2 else []
            Dp.op("sync", lambda e, tg=tg: e.dma_start(
                xnring[tg % 2][:, :].rearrange("p (c q) -> p c q", c=8),
                d_xnd[:, :].rearrange("p (c q) -> p c q", c=8)[:, :, tg * 256:(tg + 1) * 256]),
                waits=waits, incs=[(g("sxnr%d" % (tg % 2)), 16)])
        for qt2 in range(2):
            qt = 2 * tg + qt2
            waits = [(g("epf"), 2 * (qt - 1))] if qt >= 2 else [(g("dmm"), 32)]
            Dp.op("sync", lambda e, qt=qt: e.dma_start(
                xres[qt % 2][:, :], d_x[qt * 128:(qt + 1) * 128, :]),
                waits=waits, incs=[(g("sxre%d" % (qt % 2)), 16)])

    for tg in range(8):
        if tg == 0:
            emit_ring_loads(0)
        def emit_seeds(tg=tg):
            for qt2 in range(2):
                for oh in range(2):
                    b = qt2 * 2 + oh
                    waits = [(g("ssgr%d" % qt2), 16 * (tg + 1))]
                    if tg >= 1:
                        waits.append((g("epf"), 4 * (tg - 1) + b + 1))
                    else:
                        waits.append((g("dev"), 29 + b))
                    Dp.op("tensor", lambda e, b=b, qt2=qt2, oh=oh: e.matmul(
                        POUT[b][:, :], identB[:, :], sgring[qt2][:, oh * 512:(oh + 1) * 512],
                        start=True, stop=False), waits=waits)
                    waits2 = [(g("sbias"), 48)] if (tg == 0 and b == 0) else []
                    incs = [(g("esd"), 1)] if b == 3 else []
                    Dp.op("tensor", lambda e, b=b, oh=oh: e.matmul(
                        POUT[b][:, :], ones2[0:2, :], bb2[0:2, oh * 512:(oh + 1) * 512],
                        start=False, stop=False), waits=waits2, incs=incs)
        def emit_p2(m, tg=tg):
            for qt2 in range(2):
                for oh in range(2):
                    b = qt2 * 2 + oh
                    waits = []
                    if qt2 == 0 and oh == 0:
                        waits = [(g("egel"), 32 * tg + m + 1)]
                        if tg == 0:
                            waits.append((g("d2fw2"), 4 * (m // 8) + 4))
                    incs = [(g("ep2"), 1)] if b == 3 else []
                    Dp.op("tensor", lambda e, m=m, qt2=qt2, oh=oh, b=b: e.matmul(
                        POUT[b][:, :],
                        hgel[:, m * 256 + qt2 * 128: m * 256 + qt2 * 128 + 128],
                        w2t_sb[:, m * 1024 + oh * 512: m * 1024 + (oh + 1) * 512],
                        start=False, stop=(m == 31)), waits=waits, incs=incs)
        for m in range(32):
            for k in range(8):
                waits = []
                if k == 0:
                    waits = [(g("egel"), 32 * tg + m - 1)]
                    if m == 0:
                        waits.append((g("sxnr%d" % (tg % 2)), 16 * (tg // 2 + 1)))
                    if tg == 0:
                        waits.append((g("d2f"), m // 8 + 1))
                        if m == 0:
                            waits.append((g("sw1"), 128))
                incs = [(g("ep1"), 1)] if k == 7 else []
                Dp.op("tensor", lambda e, tg=tg, m=m, k=k: e.matmul(
                    PH[m % 2][:, :],
                    w1t_sb[:, k * 4096 + m * 128: k * 4096 + (m + 1) * 128],
                    xnring[tg % 2][:, k * 256:(k + 1) * 256],
                    start=(k == 0), stop=(k == 7)), waits=waits, incs=incs)
            if tg == 0 and m in (1, 9, 17, 25):
                emit_w2_group()
            if m == 2:
                emit_seeds()
            if m >= 3:
                emit_p2(m - 3)
            gw = [(g("ep1"), 32 * tg + m + 1), (g("ep2"), 32 * (tg - 1) + m + 1)]
            Dp.op("scalar", lambda e, tg=tg, m=m: e.activation(
                hgel[:, m * 256:(m + 1) * 256], PH[m % 2][:, :], AF.Gelu, bias=b1B[:, m:m + 1]),
                waits=gw, incs=[(g("egel"), 1)])
        if tg + 1 < 8:
            emit_ring_loads(tg + 1)
        for mm_ in (29, 30, 31):
            emit_p2(mm_)
        for qt2 in range(2):
            qt = 2 * tg + qt2
            for oh in range(2):
                b = qt2 * 2 + oh
                waits = [(g("ep2"), 32 * tg + 32), (g("sxre%d" % (qt % 2)), 16 * (qt // 2 + 1))]
                if oh == 0 and qt >= 2:
                    waits.append((g("sout%d" % (qt % 2)), 16 * (qt // 2)))
                Dp.op("vector", lambda e, qt=qt, oh=oh, b=b: e.tensor_tensor(
                    outsb[qt % 2][:, oh * 512:(oh + 1) * 512], POUT[b][:, :],
                    xres[qt % 2][:, oh * 512:(oh + 1) * 512], OP.add),
                    waits=waits, incs=[(g("epf"), 1)])
            Dp.op("sync", lambda e, qt=qt: e.dma_start(
                d_out[qt * 128:(qt + 1) * 128, :], outsb[qt % 2][:, :]),
                waits=[(g("epf"), 2 * (qt + 1))], incs=[(g("sout%d" % (qt % 2)), 16)])
    Dp.op("sync", lambda e: e.wait_ge(g("sout0").h, 16 * 8))
    Dp.op("sync", lambda e: e.wait_ge(g("sout1").h, 16 * 8))

    import contextlib
    with contextlib.ExitStack() as es:
        PA = [es.enter_context(nc.psum_tensor(f"pa{i}", [128, 512], BF16)).ap() for i in range(4)]
        PB = [es.enter_context(nc.psum_tensor(f"pb{i}", [128, 512], F32)).ap() for i in range(4)]
        A.emit(nc)
    with contextlib.ExitStack() as es:
        PSC = [es.enter_context(nc.psum_tensor(f"psc{i}", [128, 512], F32)).ap() for i in range(4)]
        PAV = [es.enter_context(nc.psum_tensor(f"pav{i}", [128, 512], F32)).ap() for i in range(4)]
        C.emit(nc)
    with contextlib.ExitStack() as es:
        PE2 = es.enter_context(nc.psum_tensor("pe2", [128, 1024], F32)).ap()
        PH = [es.enter_context(nc.psum_tensor(f"ph{i}", [128, 256], F32)).ap() for i in range(2)]
        POUT = [es.enter_context(nc.psum_tensor(f"pout{i}", [128, 512], F32)).ap() for i in range(4)]
        Dp.emit(nc)
    return nc


def make_consts():
    ki = np.arange(128, dtype=np.float64)
    cols = []
    for h in range(H):
        for kb in range(NKB):
            cols.append(SLOPES[h] * (kb * 128 + ki))
    alibi = np.stack(cols, axis=1).astype(np.float32)
    qpos = np.stack([qt * 128 + ki for qt in range(16)], axis=1).astype(np.float32)
    identB = np.eye(128, dtype=np.float32).astype(NPBF)
    kk, qq = np.meshgrid(np.arange(128), np.arange(128), indexing="ij")
    minmask = np.where(kk <= qq, 1e38, 0.0).astype(np.float32).astype(NPBF)
    amask = np.where(kk <= qq, 0.0, -240000.0).astype(np.float32)
    onescol = np.ones((1, 128), dtype=np.float32).astype(NPBF)
    return alibi, qpos, identB, minmask, amask, onescol


_NC_CACHE = {}

def kernel(L0_attn_x, L0_attn_Wq, L0_attn_bq, L0_attn_Wk, L0_attn_bk, L0_attn_Wv,
           L0_attn_bv, L0_attn_Wo, L0_attn_bo, L0_ffn_W1, L0_ffn_b1, L0_ffn_W2,
           L0_ffn_b2, L0_ffn_scale):
    x = np.ascontiguousarray(np.asarray(L0_attn_x, dtype=np.float32))
    wq = np.ascontiguousarray(np.asarray(L0_attn_Wq, dtype=np.float32))
    wk = np.ascontiguousarray(np.asarray(L0_attn_Wk, dtype=np.float32))
    wv = np.ascontiguousarray(np.asarray(L0_attn_Wv, dtype=np.float32))
    wo = np.ascontiguousarray(np.asarray(L0_attn_Wo, dtype=np.float32))
    w1t = np.ascontiguousarray(np.asarray(L0_ffn_W1, dtype=np.float32).T)
    w2t = np.ascontiguousarray(np.asarray(L0_ffn_W2, dtype=np.float32).T)
    bqB = np.ascontiguousarray(np.asarray(L0_attn_bq, dtype=np.float32).reshape(8, 128).T)
    bkB = np.ascontiguousarray(np.asarray(L0_attn_bk, dtype=np.float32).reshape(8, 128).T)
    b1B = np.ascontiguousarray(np.asarray(L0_ffn_b1, dtype=np.float32).reshape(32, 128).T)
    bv = np.ascontiguousarray(np.asarray(L0_attn_bv, dtype=np.float32).reshape(1, D))
    bo = np.ascontiguousarray(np.asarray(L0_attn_bo, dtype=np.float32).reshape(1, D))
    b2 = np.ascontiguousarray(np.asarray(L0_ffn_b2, dtype=np.float32).reshape(1, D))
    scl = np.zeros((1, 8), dtype=np.float32)
    scl[0, 0] = np.asarray(L0_ffn_scale, dtype=np.float32)
    alibi, qpos, identB, minmask, amask, onescol = make_consts()

    nc = build_nc()
    selmat = np.zeros((16, 2048), dtype=np.float32)
    for r in range(16):
        selmat[r, r * 128:(r + 1) * 128] = 1.0
    selmat = selmat.astype(NPBF)
    common = dict(wq=wq, wk=wk, wv=wv, wo=wo, w1t=w1t, w2t=w2t, bqB=bqB, bkB=bkB,
                  b1B=b1B, bv=bv, bo=bo, b2=b2, scl=scl, alibi=alibi, qpos=qpos,
                  identb=identB, minmask=minmask, amask=amask, onescol=onescol, selmat=selmat)
    in_maps = [dict(common, x=np.ascontiguousarray(x[b])) for b in range(8)]
    res = run_bass_kernel_spmd(nc, in_maps, core_ids=list(range(8)))
    return np.stack([np.asarray(res.results[b]["out"]) for b in range(8)], axis=0)




# version 52 (final)
# speedup vs baseline: 1.0488x; 1.0483x over previous
import numpy as np
import ml_dtypes
import concourse.bass as bass
import concourse.mybir as mybir
from concourse.bass_utils import run_bass_kernel_spmd

F32 = mybir.dt.float32
BF16 = mybir.dt.bfloat16
AF = mybir.ActivationFunctionType
OP = mybir.AluOpType
AX = mybir.AxisListType
NPBF = ml_dtypes.bfloat16

S, D, H, HD, FH = 2048, 1024, 16, 64, 4096
NQT, NKB = 16, 16
EPS = 1e-5
SLOPES = [2.0 ** (-0.5 * (h + 1)) for h in range(H)]
DMAX = [min(15, int((65.0 / SLOPES[h] + 127.0) // 128)) for h in range(H)]

def head_sched(h):
    dm = DMAX[h]
    spans = []
    chunks = []
    for kb in range(NKB):
        q0 = kb * 128
        qend = min(S, 128 * (kb + dm + 1))
        spans.append((q0, qend))
        ch = []
        c0 = q0
        while c0 < qend:
            c1 = min(qend, (c0 // 512 + 1) * 512)
            ch.append((c0, c1))
            c0 = c1
        chunks.append(ch)
    kmin = [max(0, 4 * qg - dm) for qg in range(4)]
    kmax = [min(15, 4 * qg + 3) for qg in range(4)]
    return spans, chunks, kmin, kmax

SCHED = [head_sched(h) for h in range(H)]
NCHUNK = [sum(len(c) for c in SCHED[h][1]) for h in range(H)]
CHUNKS_BEFORE = [0] * (H + 1)
for h in range(H):
    CHUNKS_BEFORE[h + 1] = CHUNKS_BEFORE[h] + NCHUNK[h]


class Sem:
    def __init__(self, nc, name):
        self.h = nc.alloc_semaphore(name=name)
        self.n = 0
    def bump(self, k=1):
        self.n += k
        return self.n


class Plan:
    ENGS = ("tensor", "vector", "scalar", "sync", "gpsimd")
    def __init__(self):
        self.ops = {e: [] for e in self.ENGS}
    def op(self, eng, fn, waits=(), incs=()):
        ws = [(s.h, int(v)) for (s, v) in waits if v > 0]
        ks = []
        for (s, k) in incs:
            s.bump(k)
            ks.append((s.h, k))
        self.ops[eng].append((fn, ws, ks))
    def emit(self, nc):
        with nc.allow_low_precision(reason="bf16 matmul operands by design; fp32 accumulation in psum"), nc.Block() as block:
            for eng in self.ENGS:
                ops = self.ops[eng]
                if not ops:
                    continue
                def run(engine, _ops=ops):
                    for fn, ws, ks in _ops:
                        for h, v in ws:
                            engine.wait_ge(h, v)
                        inst = fn(engine)
                        if ks:
                            assert inst is not None
                            inst.then_inc(*ks[0])
                            for h, k in ks[1:]:
                                engine.nop().then_inc(h, k)
                getattr(block, eng)(run)


def build_nc(dbg=False, nheads=H, dumps=()):
    nc = bass.Bass("TRN2", target_bir_lowering=False, detect_race_conditions=False)

    d_x   = nc.dram_tensor("x",    [S, D],   F32, kind="ExternalInput").ap()
    d_wq  = nc.dram_tensor("wq",   [D, D],   F32, kind="ExternalInput").ap()
    d_wk  = nc.dram_tensor("wk",   [D, D],   F32, kind="ExternalInput").ap()
    d_wv  = nc.dram_tensor("wv",   [D, D],   F32, kind="ExternalInput").ap()
    d_wo  = nc.dram_tensor("wo",   [D, D],   F32, kind="ExternalInput").ap()
    d_w1t = nc.dram_tensor("w1t",  [D, FH],  F32, kind="ExternalInput").ap()
    d_w1tb = nc.dram_tensor("w1t_bf16_stage", [128, 8 * 4096], BF16, kind="Internal").ap()
    d_w2t = nc.dram_tensor("w2t",  [FH, D],  F32, kind="ExternalInput").ap()
    d_bqB = nc.dram_tensor("bqB",  [128, 8], F32, kind="ExternalInput").ap()
    d_bkB = nc.dram_tensor("bkB",  [128, 8], F32, kind="ExternalInput").ap()
    d_b1B = nc.dram_tensor("b1B",  [128, 32], F32, kind="ExternalInput").ap()
    d_bv  = nc.dram_tensor("bv",   [1, D],   F32, kind="ExternalInput").ap()
    d_bo  = nc.dram_tensor("bo",   [1, D],   F32, kind="ExternalInput").ap()
    d_b2  = nc.dram_tensor("b2",   [1, D],   F32, kind="ExternalInput").ap()
    d_scl = nc.dram_tensor("scl",  [1, 8],   F32, kind="ExternalInput").ap()
    d_ali = nc.dram_tensor("alibi",[128, 256], F32, kind="ExternalInput").ap()
    d_qps = nc.dram_tensor("qpos", [128, 16], F32, kind="ExternalInput").ap()
    d_vc  = nc.dram_tensor("vconst", [16, 2048], BF16, kind="ExternalInput").ap()
    d_idb = nc.dram_tensor("identb", [128, 128], BF16, kind="ExternalInput").ap()
    d_mmk = nc.dram_tensor("minmask", [128, 128], BF16, kind="ExternalInput").ap()
    d_amk = nc.dram_tensor("amask", [128, 128], F32, kind="ExternalInput").ap()
    d_one = nc.dram_tensor("onescol", [1, 128], BF16, kind="ExternalInput").ap()
    d_sel = nc.dram_tensor("selmat", [16, 2048], BF16, kind="ExternalInput").ap()
    d_out = nc.dram_tensor("out",  [S, D],   F32, kind="ExternalOutput").ap()
    kindd = "ExternalOutput" if dbg else "Internal"
    d_xnd = nc.dram_tensor("xn_scratch", [128, 16384], BF16, kind=kindd).ap()
    d_std = nc.dram_tensor("stage_scratch", [128, 16384], BF16, kind=kindd).ap()

    base = None
    arena = nc.alloc_sbuf_tensor("arena", [128, 210688], mybir.dt.uint8)
    base = nc.lookup_mloc(arena).addr

    def at(name, off, shape, dt):
        return nc.alloc_sbuf_tensor_at(name, shape, dt, offset=base + off).ap()

    xn_FM  = at("xn_FM",  0,     [128, 16384], BF16)
    qaug   = [at(f"qaug{i}", 0 + 4096 * i, [65, 2048], BF16) for i in range(2)]
    kaug   = [at(f"kaug{i}", 8192 + 4096 * i, [65, 2048], BF16) for i in range(2)]
    P_T    = ([at(f"pt{i}", 16384 + 4096 * i, [128, 2048], BF16) for i in range(2)]
              + [at(f"pt{i}", 177920 + 4096 * (i - 2), [128, 2048], BF16) for i in range(2, 4)])
    accS   = [at(f"accs{i}", 24576 + 1024 * i, [64, 512], BF16) for i in range(2)]
    vT_sb  = at("vt_sb", 188672, [4, 512], BF16)
    m_all  = at("m_all", 78784, [128, 16], F32)
    vbtmp  = at("vbtmp", 78848, [128, 16], F32)
    vb     = at("vb",    78912, [128, 16], BF16)
    hgel   = at("hgel",  0,     [128, 8192], BF16)
    xnring = [at(f"xnr{i}", 16384 + 4096 * i, [128, 2048], BF16) for i in range(2)]
    sgring = [at(f"sgr{i}", 24576 + 2048 * i, [128, 1024], BF16) for i in range(2)]
    xring  = [at(f"xr{i}", 32768 + 8192 * i, [128, 2048], F32) for i in range(3)]
    sqt    = [at(f"sq{i}", 57344 + 2048 * i, [128, 1024], BF16) for i in range(2)]
    xntm   = [at(f"xtm{i}", 61440 + 2048 * i, [128, 1024], BF16) for i in range(2)]
    attnFM = at("attnFM", 32768, [128, 16384], BF16)
    xres   = [at(f"xres{i}", 32768 + 4096 * i, [128, 1024], F32) for i in range(2)]
    outsb  = [at(f"osb{i}", 40960 + 4096 * i, [128, 1024], F32) for i in range(2)]

    alibi  = at("alibi_sb", 65536, [128, 256], F32)
    qpos   = at("qpos_sb", 66560, [128, 16], F32)
    identB = at("identb_sb", 66624, [128, 128], BF16)
    minmask= at("minmask_sb", 66880, [128, 128], BF16)
    onescol= at("onescol_sb", 67136, [1, 128], BF16)
    lora   = at("loratoks", 67392, [16, 1024], BF16)
    bqB    = at("bqB_sb", 69440, [128, 8], F32)
    bkB    = at("bkB_sb", 69472, [128, 8], F32)
    b1B    = at("b1B_sb", 69504, [128, 32], F32)
    sclsb  = at("scl_sb", 69632, [1, 8], F32)
    bvbf   = at("bvbf", 69664, [1, 1024], BF16)
    bb2    = at("bb2", 71712, [2, 1024], BF16)
    ones2  = at("ones2", 73760, [2, 128], BF16)
    oddcT  = at("oddcT", 75808, [128, 128], F32)
    lnst   = at("lnst", 76320, [128, 96], F32)
    scalebf = at("scalebf", 76672, [1, 8], BF16)
    scale_rep = at("scale_rep", 76704, [128, 8], F32)
    evr1 = [at(f"evr1_{i}", 76736 + 1024 * i, [128, 512], BF16) for i in range(2)]
    evr2 = [at(f"evr2_{i}", 76736 + 512 * i, [128, 256], BF16) for i in range(2)]
    amask  = at("amask_sb", 186112, [128, 128], F32)
    lnrow  = at("lnrow", 186624, [1, 512], F32)
    recB   = at("recB", 27904, [1, 2048], BF16)
    zcol   = at("zcol", 32000, [128, 72], BF16)
    selmat = at("selmat", 28672, [16, 2048], BF16)
    Q_FM   = at("Q_FM", 79104, [128, 16384], BF16)
    K_FM   = at("K_FM", 111872, [128, 16384], BF16)
    w1t_sb = at("w1t_sb", 79104, [128, 32768], BF16)
    V_pad  = at("V_pad", 144640, [128, 16640], BF16)
    wbuf   = [at(f"wbuf{i}", 177920 + 16384 * i, [128, 8192], BF16) for i in range(2)]
    w2t_sb = at("w2t_sb", 144640, [128, 32768], BF16)
    dstage = at("dstage", 210688, [128, 1024], BF16)

    sem_names = ["sx0", "sx0h", "sx1", "sx2", "smisc", "swq", "swk", "swv", "swo", "sw1", "sw2", "slt",
                 "saugq0", "saugq1", "saugk0", "saugk1", "saugv0", "saugv1",
                 "sxnr0", "sxnr1", "ssgr0", "ssgr1", "sxre0", "sxre1", "sout0", "sout1", "sw2b", "samk", "sdst0", "sdst1",
                 "sxnsp", "sxnsp2", "sxnsp3", "sxnsp4", "sw1s", "saug", "sxnr", "ssgr", "sdst", "sxre", "sout",
                 "aln_v", "aln_s", "aln_sq", "asum", "axn", "atp", "axnE",
                 "bmm", "bev",
                 "cdg", "cdgE", "cvb", "cvt", "cvtE", "csc", "cexp", "cmsk",
                 "cavk", "cavq", "crec", "crep", "cacc", "catt", "cmax",
                 "dmm", "dev",
                 "d2oc", "d2ocE", "d2sc", "d2er", "d2erE", "d2f", "d2fw2", "d2ev", "d2evw2",
                 "ep1", "egel", "ep2", "epf", "esd", "ssel", "sbias", "cpm", "vch"]
    SM = {n: Sem(nc, n) for n in sem_names}
    g = lambda n: SM[n]

    A = Plan()
    A.op("sync", lambda e: e.dma_start(
        xring[0][:, 0:1024], d_x[0:128, :]), incs=[(g("sx0h"), 16)])
    A.op("sync", lambda e: e.dma_start(
        xring[0][:, 1024:2048], d_x[128:256, :]), incs=[(g("sx0"), 16)])
    A.op("sync", lambda e: e.dma_start(
        xring[1][:, :].rearrange("p (j d) -> p j d", j=2),
        d_x[256:512, :].rearrange("(j p) d -> p j d", p=128)),
        incs=[(g("sx1"), 16)])
    for (dst, srcap) in [(alibi, d_ali), (qpos, d_qps), (identB, d_idb), (minmask, d_mmk),
                         (onescol, d_one), (bqB, d_bqB), (bkB, d_bkB), (b1B, d_b1B),
                         (sclsb, d_scl)]:
        A.op("sync", lambda e, a=dst, b=srcap: e.dma_start(a[:, :], b[:, :]),
             incs=[(g("smisc"), 16)])
    for (dst, srcap) in [(bvbf, d_bv), (bb2[0:1, :], d_bo), (bb2[1:2, :], d_b2)]:
        A.op("gpsimd", lambda e, a=dst, b=srcap: e.dma_start(a[:, :], b[:, :]),
             incs=[(g("sbias"), 16)])
    A.op("gpsimd", lambda e: e.dma_start(
        wbuf[0][:, :].rearrange("p (c o) -> p c o", c=8),
        d_wq.rearrange("(c p) o -> p c o", p=128)), incs=[(g("swq"), 16)])
    A.op("gpsimd", lambda e: e.dma_start(
        wbuf[1][:, :].rearrange("p (c o) -> p c o", c=8),
        d_wk.rearrange("(c p) o -> p c o", p=128)), incs=[(g("swk"), 16)])
    for c in range(8):
        w1cw = [(g("axn"), 8)] if c == 0 else []
        A.op("gpsimd", lambda e, c=c: e.dma_start(
            d_w1tb[:, c * 4096:(c + 1) * 4096], d_w1t[c * 128:(c + 1) * 128, :]),
            waits=w1cw, incs=[(g("sw1s"), 16)])
    vones = V_pad[:, :].rearrange("p (t h e) -> p t h e", t=16, h=16)[:, :, :, 64:65]
    A.op("vector", lambda e: e.memset(vones, 1.0))
    A.op("vector", lambda e: e.memset(ones2[:, :], 1.0))
    for pr in range(2, 8):
        waits = [(g("axn"), 2 * pr - 4)] if pr >= 3 else []
        A.op("sync", lambda e, pr=pr: e.dma_start(
            xring[pr % 3][:, :].rearrange("p (j d) -> p j d", j=2),
            d_x[pr * 256:(pr + 1) * 256, :].rearrange("(j p) d -> p j d", p=128)),
            waits=waits, incs=[(g("sx%d" % (pr % 3)), 16)])
        if pr == 3:
            A.op("sync", lambda e: e.dma_start(lora[:, :], xntm[0][1:17, :]),
                 waits=[(g("axn"), 1)], incs=[(g("slt"), 16)])
    use = 0

    def xwait(qt):
        if qt == 0:
            return (g("sx0h"), 16)
        return (g("sx%d" % ((qt // 2) % 3)), 16 * ((qt // 2) // 3 + 1))

    def emit_square(q2):
        xr2 = xring[(q2 // 2) % 3][:, (q2 % 2) * 1024:(q2 % 2) * 1024 + 1024]
        waits = [xwait(q2)]
        if q2 >= 2:
            waits.append((g("asum"), q2 - 1))
        A.op("scalar", lambda e, xr2=xr2, q2=q2: e.activation(sqt[q2 % 2][:, :], xr2, AF.Square),
             waits=waits, incs=[(g("aln_sq"), 1)])

    def emit_tile_sums(qt):
        emit_square(qt)
        xr = xring[(qt // 2) % 3][:, (qt % 2) * 1024:(qt % 2) * 1024 + 1024]
        A.op("vector", lambda e, xr=xr, qt=qt: e.reduce_sum(lnst[:, qt:qt + 1], xr, AX.X),
             waits=[xwait(qt)])
        A.op("vector", lambda e, qt=qt: e.reduce_sum(lnst[:, 16 + qt:17 + qt], sqt[qt % 2][:, :], AX.X),
             waits=[(g("aln_sq"), qt + 1)], incs=[(g("asum"), 1)])

    def emit_ln_a(gq):
        if gq < 2:
            for qt in range(4 * gq, 4 * gq + 4):
                emit_tile_sums(qt)
        base = 32 + 28 * (gq % 2)
        mean4 = lnst[:, base:base + 4]; t4 = lnst[:, base + 4:base + 8]
        msq4 = lnst[:, base + 8:base + 12]; vpe4 = lnst[:, base + 12:base + 16]
        sq4 = lnst[:, base + 16:base + 20]; r4 = lnst[:, base + 20:base + 24]
        negmr4 = lnst[:, base + 24:base + 28]
        A.op("vector", lambda e, gq=gq, o=mean4: e.tensor_scalar(o, lnst[:, 4 * gq:4 * gq + 4], 1.0 / 1024, None, OP.mult),
             waits=[(g("vch"), g("vch").n), (g("axn"), 4 * gq - 4)], incs=[(g("vch"), 1)])
        A.op("vector", lambda e, gq=gq, o=t4: e.tensor_scalar(o, lnst[:, 16 + 4 * gq:20 + 4 * gq], 1.0 / 1024, EPS, OP.mult, OP.add),
             waits=[(g("vch"), g("vch").n)], incs=[(g("vch"), 1)])
        A.op("vector", lambda e, i=mean4, o=msq4: e.tensor_tensor(o, i, i, OP.mult),
             waits=[(g("vch"), g("vch").n)], incs=[(g("vch"), 1)])
        A.op("vector", lambda e, a=t4, b=msq4, o=vpe4: e.tensor_tensor(o, a, b, OP.subtract),
             waits=[(g("vch"), g("vch").n)], incs=[(g("aln_v"), 1), (g("vch"), 1)])
        A.op("scalar", lambda e, i=vpe4, o=sq4: e.activation(o, i, AF.Sqrt),
             waits=[(g("aln_v"), 2 * gq + 1)], incs=[(g("aln_s"), 1)])
        A.op("vector", lambda e, i=sq4, o=r4: e.reciprocal(o, i),
             waits=[(g("aln_s"), gq + 1)], incs=[(g("vch"), 1)])
        A.op("vector", lambda e, i=mean4, r=r4, o=negmr4: e.scalar_tensor_tensor(
            o, i, -1.0, r, OP.mult, OP.mult),
            waits=[(g("vch"), g("vch").n)], incs=[(g("aln_v"), 1), (g("vch"), 1)])
        def emit_act(qt):
            xr = xring[(qt // 2) % 3][:, (qt % 2) * 1024:(qt % 2) * 1024 + 1024]
            j4c = qt - 4 * gq
            waits = [(g("aln_v"), 2 * gq + 2), (g("atp"), 2 * (qt - 1))]
            if qt == 2:
                waits.append((g("slt"), 16))
            A.op("scalar", lambda e, xr=xr, qt=qt, r=r4[:, j4c:j4c + 1], b=negmr4[:, j4c:j4c + 1]: e.activation(
                xntm[qt % 2][:, :], xr, AF.Identity, bias=b, scale=r),
                waits=waits, incs=[(g("axn"), 1)])
        for qt in range(4 * gq, 4 * gq + 2):
            emit_act(qt)
        acts_b[gq] = emit_act
        return

    acts_b = {}

    def emit_ln_b(gq):
        for qt in range(4 * gq + 2, 4 * gq + 4):
            acts_b[gq](qt)
        for qt in range(4 * gq, 4 * gq + 4):
            for b in range(2):
                for j4 in range(4):
                    j = b * 4 + j4
                    waits = []
                    if j == 0:
                        waits = [(g("axn"), qt + 1), (g("axnE"), 2 * (qt - 1))]
                        if qt == 0:
                            waits.append((g("smisc"), 144))
                    incs = [(g("atp"), 1)] if j4 == 3 else []
                    A.op("tensor", lambda e, qt=qt, b=b, j4=j4, j=j: e.transpose(
                        PA[(qt % 2) * 2 + b][:, j4 * 128:(j4 + 1) * 128],
                        xntm[qt % 2][:, j * 128:(j + 1) * 128], identB[:, :]),
                        waits=waits, incs=incs)
                dst = xn_FM[:, :].rearrange("p (c q) -> p c q", c=8)[:, b * 4:(b + 1) * 4, qt * 128:(qt + 1) * 128]
                src_fn = lambda e, qt=qt, b=b, dst=dst: e.tensor_copy(
                    dst, PA[(qt % 2) * 2 + b][:, :].rearrange("p (c q) -> p c q", c=4))
                A.op("vector", src_fn, waits=[(g("atp"), 2 * qt + b + 1)], incs=[(g("axnE"), 1)])

    def emit_qk(gq):
        nonlocal use
        for (wsem, wsb, dstT, bias) in [("swq", 0, Q_FM, bqB), ("swk", 1, K_FM, bkB)]:
            for c in range(8):
                for k in range(8):
                    waits = []
                    if k == 0:
                        waits = [(g("bev"), use - 3)]
                        if c == 0:
                            waits.append((g("axnE"), 8 * gq + 8))
                            waits.append((g(wsem), 16))
                    incs = [(g("bmm"), 1)] if k == 7 else []
                    A.op("tensor", lambda e, u=use, w=wsb, c=c, gq=gq, k=k: e.matmul(
                        PB[u % 4][:, :],
                        wbuf[w][:, k * 1024 + c * 128: k * 1024 + (c + 1) * 128],
                        xn_FM[:, k * 2048 + gq * 512: k * 2048 + (gq + 1) * 512],
                        start=(k == 0), stop=(k == 7)),
                        waits=waits, incs=incs)
                bw = [(g("bmm"), use + 1)]
                if use == 0:
                    bw.append((g("smisc"), 144))
                A.op("scalar", lambda e, u=use, dstT=dstT, bias=bias, c=c, gq=gq: e.activation(
                    dstT[:, c * 2048 + gq * 512: c * 2048 + (gq + 1) * 512],
                    PB[u % 4][:, :], AF.Identity, bias=bias[:, c:c + 1]),
                    waits=bw, incs=[(g("bev"), 1)])
                use += 1
                if wsb == 0 and c % 2 == 1 and gq + 2 < 4:
                    emit_tile_sums(4 * (gq + 2) + (c - 1) // 2)

    emit_ln_a(0)
    emit_ln_b(0)
    emit_ln_a(1)
    for gq in range(4):
        emit_qk(gq)
        if gq + 1 < 4:
            emit_ln_b(gq + 1)
        if gq + 2 < 4:
            emit_ln_a(gq + 2)
    A.op("gpsimd", lambda e: e.dma_start(
        wbuf[0][:, :].rearrange("p (c o) -> p c o", c=8),
        d_wv.rearrange("(c p) o -> p c o", p=128)),
        waits=[(g("bmm"), 56)], incs=[(g("swv"), 16)])
    A.op("gpsimd", lambda e: e.dma_start(
        wbuf[1][:, :].rearrange("p (c o) -> p c o", c=8),
        d_wo.rearrange("(c p) o -> p c o", p=128)),
        waits=[(g("bmm"), 64)], incs=[(g("swo"), 16)])
    for qt in range(NQT):
        for vh in range(2):
            waits = [(g("bev"), use - 3)]
            if qt == 0 and vh == 0:
                waits.append((g("swv"), 16))
                waits.append((g("sbias"), 48))
            A.op("tensor", lambda e, u=use, vh=vh: e.matmul(
                PB[u % 4][:, :], onescol[0:1, :], bvbf[0:1, vh * 512:(vh + 1) * 512],
                start=True, stop=False), waits=waits)
            for k in range(8):
                incs = [(g("bmm"), 1)] if k == 7 else []
                A.op("tensor", lambda e, u=use, qt=qt, vh=vh, k=k: e.matmul(
                    PB[u % 4][:, :],
                    xn_FM[:, k * 2048 + qt * 128: k * 2048 + (qt + 1) * 128],
                    wbuf[0][:, k * 1024 + vh * 512: k * 1024 + (vh + 1) * 512],
                    start=False, stop=(k == 7)), incs=incs)
            dst = V_pad[:, qt * 1040 + vh * 8 * 65: qt * 1040 + (vh + 1) * 8 * 65].rearrange(
                "p (h s) -> p h s", h=8)[:, :, 0:64]
            A.op("scalar", lambda e, u=use, dst=dst: e.activation(
                dst, PB[u % 4][:, :].rearrange("p (h s) -> p h s", h=8), AF.Identity),
                waits=[(g("bmm"), use + 1)], incs=[(g("bev"), 1)])
            use += 1

    A.op("sync", lambda e: e.dma_start(d_xnd[:, 0:4096], xn_FM[:, 0:4096]),
         waits=[(g("axnE"), 32)], incs=[(g("sxnsp"), 16)])
    A.op("scalar", lambda e: e.dma_start(d_xnd[:, 4096:8192], xn_FM[:, 4096:8192]),
         waits=[(g("axnE"), 32)], incs=[(g("sxnsp2"), 16)])
    A.op("gpsimd", lambda e: e.dma_start(d_xnd[:, 12288:16384], xn_FM[:, 12288:16384]),
         waits=[(g("axnE"), 32)], incs=[(g("sxnsp4"), 16)])
    A.op("gpsimd", lambda e: e.dma_start(d_xnd[:, 8192:12288], xn_FM[:, 8192:12288]),
         incs=[(g("sxnsp3"), 16)])
    A.op("vector", lambda e: e.memset(kaug[0][64:65, :], 1.0), waits=[(g("sxnsp2"), 16)])
    A.op("vector", lambda e: e.memset(kaug[1][64:65, :], 1.0))
    A.op("vector", lambda e: e.memset(zcol[:, :], 0.0), waits=[(g("sxnsp4"), 16)])

    C = Plan()
    C.op("gpsimd", lambda e: e.dma_start(amask[:, :], d_amk[:, :]), incs=[(g("samk"), 16)])

    slot_consumer = []
    csc_after = []
    nexp = [0]
    nmax = [0]

    def slot_producer_waits(s):
        w = []
        if s >= 4:
            w.append((g(slot_consumer[s - 4][0]), slot_consumer[s - 4][1]))
        return w

    exp_after = {}
    csc_after_head = {}

    def emit_diag_slots(hh, s0):
        c2, r0 = hh >> 1, 64 * (hh & 1)
        for half in range(4):
            s = s0 + half
            w0 = slot_producer_waits(s)
            for j in range(4):
                qt = 4 * half + j
                waits = list(w0) if j == 0 else []
                incs = [(g("csc"), 1)] if j == 3 else []
                C.op("tensor", lambda e, c2=c2, r0=r0, qt=qt, j=j, s=s: e.matmul(
                    PSC[s % 4][:, j * 128:(j + 1) * 128],
                    Q_FM[r0:r0 + 64, c2 * 2048 + qt * 128: c2 * 2048 + (qt + 1) * 128],
                    K_FM[r0:r0 + 64, c2 * 2048 + qt * 128: c2 * 2048 + (qt + 1) * 128],
                    start=True, stop=True), waits=waits, incs=incs)
            csc_after.append(g("csc").n)
            C.op("vector", lambda e, half=half, s=s: e.reduce_max(
                m_all[:, half * 4:(half + 1) * 4],
                PSC[s % 4][:, :].rearrange("p (a q) -> p a q", a=4), AX.X),
                waits=[(g("csc"), csc_after[s])], incs=[(g("cmax"), 1)])
            nmax[0] += 1
            slot_consumer.append(("cmax", nmax[0]))

    def emit_vt_slot(hh, s):
        w0 = slot_producer_waits(s)
        for gg in range(4):
            waits = list(w0) + [(g("cvb"), hh + 1)] if gg == 0 else []
            incs = [(g("csc"), 1), (g("cvt"), 1)] if gg == 3 else []
            C.op("tensor", lambda e, gg=gg, s=s: e.matmul(
                PSC[s % 4][0:4, gg * 128:(gg + 1) * 128],
                vb[:, :].rearrange("p (m q) -> p q m", q=4)[:, gg, :], identB[:, :],
                start=True, stop=True), waits=waits, incs=incs)
        csc_after.append(g("csc").n)
        waits = [(g("csc"), csc_after[s])]
        if hh >= 1:
            waits.append((g("saugv%d" % ((hh - 1) % 2)), 16 * ((hh - 1) // 2 + 1)))
        C.op("vector", lambda e, s=s: e.tensor_copy(vT_sb[:, :], PSC[s % 4][0:4, 0:512]),
             waits=waits, incs=[(g("cmax"), 1), (g("cvtE"), 1)])
        nmax[0] += 1
        slot_consumer.append(("cmax", nmax[0]))

    def emit_av(oi, h, kb):
        spans, chunks, kmin, kmax = SCHED[h]
        qend = spans[kb][1]
        pt = P_T[(16 * oi + kb) % 4]
        qgs = [qg for qg in range(kb // 4, 4) if qg * 512 < qend]
        last_qg = qgs[-1]
        for qi, qg in enumerate(qgs):
            q0 = max(kb * 128, qg * 512)
            q1 = min(qend, (qg + 1) * 512)
            waits = []
            if qi == 0:
                waits.append((g("cexp"), exp_after[(h, kb)]))
                waits.append((g("cmsk"), oi * 16 + kb + 1))
            if kb == kmin[qg]:
                zw = [(g("cacc"), 4 * (oi - 1) + qg + 1), (g("crec"), 4 * (oi - 1) + qg + 1),
                      (g("catt"), 4 * (oi - 1) + qg + 1)]
                C.op("tensor", lambda e, qg=qg: e.matmul(
                    PAV[qg][0:65, 0:512], zcol[:, 0:65], V_pad[:, 0:512],
                    start=True, stop=False), waits=zw)
            stop = (kb == kmax[qg])
            incs = []
            if stop:
                incs.append((g("cavq"), 1))
            if qg == last_qg:
                incs.append((g("cavk"), 1))
            C.op("tensor", lambda e, h=h, kb=kb, qg=qg, q0=q0, q1=q1, stop=stop, pt=pt: e.matmul(
                PAV[qg][0:65, q0 - qg * 512: q1 - qg * 512],
                V_pad[:, kb * 1040 + h * 65: kb * 1040 + h * 65 + 65],
                pt[:, q0:q1], start=False, stop=stop),
                waits=waits, incs=incs)

    def emit_vb(oi, hh):
        vbw = [(g("cvt"), oi), (g("vch"), 0)]
        if oi == 0:
            vbw.append((g("smisc"), 144))
        C.op("vector", lambda e, hh=hh: e.tensor_scalar(vbtmp[:, :], qpos[:, :], -8.0 * SLOPES[hh], None, OP.mult),
             waits=vbw, incs=[(g("vch"), 1)])
        C.op("vector", lambda e: e.tensor_tensor(vb[:, :], vbtmp[:, :], m_all[:, :], OP.subtract),
             waits=[(g("vch"), g("vch").n), (g("cmax"), nmax[0])], incs=[(g("cvb"), 1)])

    pending_rn = []
    pending_pre = []

    def emit_renorm_pre(oi, qg):
        C.op("scalar", lambda e, qg=qg: e.activation(lnrow[0:1, :], PAV[qg][64:65, :], AF.Ln),
             waits=[(g("cavq"), 4 * oi + qg + 1), (g("crep"), 4 * (oi - 1) + qg + 1)])
        C.op("scalar", lambda e, qg=qg: e.activation(
            recB[0:1, qg * 512:(qg + 1) * 512], lnrow[0:1, :], AF.Exp, scale=-1.0),
            incs=[(g("crec"), 1)])

    def emit_renorm_copy(oi, qg):
        C.op("vector", lambda e, qg=qg: e.tensor_copy(
            accS[qg % 2][:, :], PAV[qg][0:64, :]),
            waits=[(g("cavq"), 4 * oi + qg + 1), (g("catt"), 4 * oi + qg - 1)],
            incs=[(g("cacc"), 1)])

    def emit_renorm_qg(oi, c2, r0, qg):
        if qg >= 2:
            emit_renorm_copy(oi, qg)
        C.op("tensor", lambda e, qg=qg: e.matmul(
            PAV[qg][0:64, 0:512], onescol[0:1, 0:64],
            recB[0:1, qg * 512:(qg + 1) * 512],
            start=True, stop=True),
            waits=[(g("crec"), 4 * oi + qg + 1), (g("cacc"), 4 * oi + qg + 1)],
            incs=[(g("crep"), 1)])
        C.op("vector", lambda e, c2=c2, r0=r0, qg=qg: e.tensor_tensor(
            attnFM[r0:r0 + 64, c2 * 2048 + qg * 512: c2 * 2048 + (qg + 1) * 512],
            accS[qg % 2][:, :], PAV[qg][0:64, 0:512], OP.mult),
            waits=[(g("crep"), 4 * oi + qg + 1), (g("cacc"), 4 * oi + qg + 1)],
            incs=[(g("catt"), 1)])

    HORD = list(range(nheads))
    pos_of = {hh: oo for oo, hh in enumerate(HORD)}

    for oi, h in enumerate(HORD):
        c2, r0 = h >> 1, 64 * (h & 1)
        spans, chunks, kmin, kmax = SCHED[h]

        waits = [(g("sxnsp"), 16)]
        if oi >= 2:
            waits.append((g("csc"), csc_after_head[oi - 2]))
        C.op("sync", lambda e, oi=oi, c2=c2, r0=r0: e.dma_start(
            qaug[oi % 2][0:64, :], Q_FM[r0:r0 + 64, c2 * 2048:(c2 + 1) * 2048]),
            waits=waits, incs=[(g("saugq%d" % (oi % 2)), 16)])
        kgw = [(g("sxnsp2"), 16)] if oi == 0 else []
        C.op("sync", lambda e, oi=oi, c2=c2, r0=r0: e.dma_start(
            kaug[oi % 2][0:64, :], K_FM[r0:r0 + 64, c2 * 2048:(c2 + 1) * 2048]),
            waits=kgw, incs=[(g("saugk%d" % (oi % 2)), 16)])
        C.op("sync", lambda e, oi=oi, h=h: e.dma_start(
            qaug[oi % 2][64:65, :], d_vc[h:h + 1, :]),
            incs=[(g("saugv%d" % (oi % 2)), 16)])
        pc = [len(chunks[kb_]) for kb_ in range(NKB)]
        tot_pc = sum(pc)
        kb_d, kb_e = NKB - 4, NKB - 3
        cum = 0
        for kb_ in range(NKB):
            cum += pc[kb_]
            if tot_pc - cum <= 14 and kb_d == NKB - 4:
                kb_d = max(1, min(kb_, NKB - 4))
            if tot_pc - cum <= 8:
                kb_e = max(kb_d + 1, min(kb_, NKB - 3))
                break
        for kb in range(NKB):
            pt = P_T[(16 * oi + kb) % 4]
            q0span, qend = spans[kb]
            pieces = []
            p0 = q0span
            while p0 < qend:
                p1 = min(qend, (p0 // 512 + 1) * 512)
                pieces.append((p0, p1))
                p0 = p1
            for pi, (p0, p1) in enumerate(pieces):
                s = len(slot_consumer)
                w0 = slot_producer_waits(s)
                if kb == 0 and pi == 0:
                    w0.append((g("saugq%d" % (oi % 2)), 16 * (oi // 2 + 1)))
                    w0.append((g("saugk%d" % (oi % 2)), 16 * (oi // 2 + 1)))
                    w0.append((g("saugv%d" % (oi % 2)), 16 * (oi // 2 + 1)))
                subs = []
                c0s = p0
                while c0s < p1:
                    c1s = min(p1, c0s + 512)
                    subs.append((c0s, c1s))
                    c0s = c1s
                for si, (c0s, c1s) in enumerate(subs):
                    waits = list(w0) if si == 0 else []
                    incs = [(g("csc"), 1)] if si == len(subs) - 1 else []
                    C.op("tensor", lambda e, oi=oi, kb=kb, c0s=c0s, c1s=c1s, p0=p0, s=s: e.matmul(
                        PSC[s % 4][:, c0s - p0: c1s - p0],
                        kaug[oi % 2][:, kb * 128:(kb + 1) * 128],
                        qaug[oi % 2][:, c0s:c1s], start=True, stop=True),
                        waits=waits, incs=incs)
                csc_after.append(g("csc").n)
                if h == 0 and pi == 0:
                    C.op("vector", lambda e, s=s, p0=p0, kb=kb: e.tensor_tensor(
                        PSC[s % 4][:, kb * 128 - p0: (kb + 1) * 128 - p0],
                        PSC[s % 4][:, kb * 128 - p0: (kb + 1) * 128 - p0], amask[:, :], OP.add),
                        waits=[(g("csc"), csc_after[s]), (g("samk"), 16)], incs=[(g("cpm"), 1)])
                waits = [(g("csc"), csc_after[s]), (g("cavk"), oi * 16 + kb - 3),
                         (g("cmsk"), oi * 16 + kb - 3)]
                if h == 0 and pi == 0:
                    waits.append((g("cpm"), kb + 1))
                if h == 0 and kb == 0 and pi == 0:
                    waits.append((g("sxnsp3"), 16))
                    waits.append((g("sxnsp4"), 16))
                    waits.append((g("smisc"), 144))
                C.op("scalar", lambda e, h=h, kb=kb, p0=p0, p1=p1, s=s, pt=pt: e.activation(
                    pt[:, p0:p1], PSC[s % 4][:, 0:p1 - p0], AF.Exp,
                    bias=alibi[:, h * 16 + kb: h * 16 + kb + 1], scale=0.125),
                    waits=waits, incs=[(g("cexp"), 1)])
                nexp[0] += 1
                slot_consumer.append(("cexp", nexp[0]))
                if pi == 0:
                    C.op("vector", lambda e, kb=kb, pt=pt: e.tensor_tensor(
                        pt[:, kb * 128:(kb + 1) * 128],
                        pt[:, kb * 128:(kb + 1) * 128], minmask[:, :], OP.min),
                        waits=[(g("cexp"), nexp[0])], incs=[(g("cmsk"), 1)])
                if pending_pre:
                    emit_renorm_pre(*pending_pre.pop(0))
                if pending_rn:
                    emit_renorm_qg(*pending_rn.pop(0))
            exp_after[(h, kb)] = nexp[0]
            if h == 0 and kb == 0:
                pass
            if kb >= 2:
                while pending_pre:
                    emit_renorm_pre(*pending_pre.pop(0))
                while pending_rn:
                    emit_renorm_qg(*pending_rn.pop(0))
                emit_av(oi, h, kb - 2)
            if kb == NKB - 1:
                csc_after_head[oi] = g("csc").n
                emit_av(oi, h, kb - 1)
                emit_av(oi, h, kb)
        while pending_pre:
            emit_renorm_pre(*pending_pre.pop(0))
        emit_renorm_pre(oi, 0)
        for qg in range(2):
            emit_renorm_copy(oi, qg)
        if oi + 1 < nheads:
            if sum(len(c) for c in SCHED[HORD[oi + 1]][1]) >= 12:
                pending_pre.extend((oi, qg) for qg in range(1, 4))
            else:
                for qg in range(1, 4):
                    emit_renorm_pre(oi, qg)
            pending_rn.extend((oi, c2, r0, qg) for qg in range(4))
        else:
            for qg in range(1, 4):
                emit_renorm_pre(oi, qg)
            for qg in range(4):
                emit_renorm_qg(oi, c2, r0, qg)

    for i in range(4):
        gate = csc_after_head[max(pos_of[hh] for hh in range(4 * i, min(4 * i + 4, nheads)))]
        for half in range(2):
            c = i if half == 0 else i + 4
            C.op("gpsimd", lambda e, c=c: e.dma_start(
                w1t_sb[:, c * 4096:(c + 1) * 4096], d_w1tb[:, c * 4096:(c + 1) * 4096]),
                waits=[(g("csc"), gate), (g("sw1s"), 128)], incs=[(g("sw1"), 16)])

    Dp = Plan()
    _dbg_tensors = dict(qaug0=qaug[0], kaug0=kaug[0], m_all=m_all, vb=vb, vT_sb=vT_sb,
                        attnFM=attnFM, pt0=P_T[0], pt1=P_T[1], recB=recB,
                        V_pad=V_pad, Q_FM=Q_FM, K_FM=K_FM, accS0=accS[0], accS1=accS[1])
    if dumps:
        SM["sdump"] = Sem(nc, "sdump")
    for _dn in dumps:
        _t = _dbg_tensors[_dn]
        _dd = nc.dram_tensor("dump_" + _dn, list(_t.shape), _t.dtype, kind="ExternalOutput").ap()
        Dp.op("sync", lambda e, a=_dd, b=_t: e.dma_start(a[:, :], b[:, :]), incs=[(g("sdump"), 16)])
    if dumps:
        Dp.op("sync", lambda e: e.wait_ge(g("sdump").h, 16 * len(dumps)))

    for ci in range(8):
        w2w = [(g("dmm"), 32)] if ci >= 6 else []
        w2sem = "sw2b" if ci >= 6 else "sw2"
        Dp.op("gpsimd", lambda e, ci=ci: e.dma_start(
            w2t_sb[:, ci * 4096:(ci + 1) * 4096].rearrange("p (q o) -> p q o", q=4),
            d_w2t[ci * 512:(ci + 1) * 512, :].rearrange("(q p) o -> p q o", p=128)),
            waits=w2w, incs=[(g(w2sem), 16)])

    Dp.op("sync", lambda e: e.dma_start(selmat[:, :], d_sel[:, :]), incs=[(g("ssel"), 16)])
    Dp.op("sync", lambda e: e.dma_start(
        xnring[0][:, :].rearrange("p (c q) -> p c q", c=8),
        d_xnd[:, :].rearrange("p (c q) -> p c q", c=8)[:, :, 0:256]),
        waits=[(g("sxnsp2"), 16), (g("sxnsp3"), 16), (g("sxnsp4"), 16)], incs=[(g("sxnr0"), 16)])

    Dp.op("vector", lambda e: e.tensor_scalar(scalebf[0:1, 0:1], sclsb[0:1, 0:1], 1.0, None, OP.mult),
         waits=[(g("smisc"), 144)], incs=[(g("d2sc"), 1)])
    for c in range(8):
        Dp.op("tensor", lambda e, c=c: e.matmul(
            PE2[:, c * 16:(c + 1) * 16], lora[:, c * 128:(c + 1) * 128], identB[0:16, 0:16],
            start=True, stop=True))
    Dp.op("tensor", lambda e: e.matmul(
        PE2[:, 128:129], onescol[0:1, :], scalebf[0:1, 0:1], start=True, stop=True),
        waits=[(g("d2sc"), 1)], incs=[(g("d2oc"), 1)])
    Dp.op("vector", lambda e: e.tensor_copy(scale_rep[:, 0:1], PE2[:, 128:129]),
         waits=[(g("d2oc"), 1)], incs=[(g("vch"), 1)])
    Dp.op("vector", lambda e: e.tensor_scalar(oddcT[:, :], PE2[:, 0:128], scale_rep[:, 0:1], None, OP.mult),
         waits=[(g("vch"), g("vch").n)], incs=[(g("d2ocE"), 1), (g("vch"), 1)])

    w1_half = [0]
    def emit_w1_half():
        i = w1_half[0]
        if i >= 8:
            return
        j, hf = i // 2, i % 2
        waits = [(g("ssel"), 16), (g("d2ocE"), 1)]
        if i >= 2:
            waits.append((g("d2ev"), i - 1))
        Dp.op("tensor", lambda e, j=j, hf=hf, i=i: e.matmul(
            PE2[:, (i % 2) * 512:(i % 2) * 512 + 512],
            selmat[:, 2 * j * 128:(2 * j + 1) * 128], lora[:, hf * 512:(hf + 1) * 512],
            start=True, stop=True), waits=waits, incs=[(g("d2er"), 1)])
        ew = [(g("d2er"), i + 1)]
        if i >= 2:
            ew.append((g("d2erE"), i - 1))
        Dp.op("scalar", lambda e, i=i: e.activation(
            evr1[i % 2][:, :], PE2[:, (i % 2) * 512:(i % 2) * 512 + 512], AF.Identity),
            waits=ew, incs=[(g("d2ev"), 1)])
        for c in range(8):
            incs = []
            if c == 7:
                incs.append((g("d2erE"), 1))
                if hf == 1:
                    incs.append((g("d2f"), 1))
            w1w = [(g("sw1"), 128)]
            if c == 0:
                w1w.append((g("d2ev"), i + 1))
            Dp.op("vector", lambda e, j=j, hf=hf, c=c, i=i: e.scalar_tensor_tensor(
                w1t_sb[:, c * 4096 + j * 1024 + hf * 512: c * 4096 + j * 1024 + hf * 512 + 512],
                evr1[i % 2][:, :],
                oddcT[:, c * 16 + 1 + 2 * j: c * 16 + 2 + 2 * j],
                w1t_sb[:, c * 4096 + j * 1024 + hf * 512: c * 4096 + j * 1024 + hf * 512 + 512],
                OP.mult, OP.add),
                waits=w1w, incs=incs)
        w1_half[0] += 1

    duse = 0
    for qt in range(NQT):
        for oh in range(2):
            for c in range(8):
                waits = []
                if c == 0:
                    waits = [(g("dev"), duse - 3)]
                    if duse == 0:
                        waits.append((g("swo"), 16))
                incs = [(g("dmm"), 1)] if c == 7 else []
                Dp.op("tensor", lambda e, u=duse, qt=qt, oh=oh, c=c: e.matmul(
                    POUT[u % 4][:, :],
                    attnFM[:, c * 2048 + qt * 128: c * 2048 + (qt + 1) * 128],
                    wbuf[1][:, c * 1024 + oh * 512: c * 1024 + (oh + 1) * 512],
                    start=(c == 0), stop=(c == 7)), waits=waits, incs=incs)
            dw = [(g("dmm"), duse + 1)]
            if qt >= 1:
                dw.append((g("sdst%d" % ((qt - 1) % 2)), 32 * ((qt - 1) // 2) + 16 * (oh + 1)))
            Dp.op("scalar", lambda e, u=duse, oh=oh: e.activation(
                dstage[:, oh * 512:(oh + 1) * 512], POUT[u % 4][:, :], AF.Identity),
                waits=dw, incs=[(g("dev"), 1)])
            duse += 1
            Dp.op("sync", lambda e, qt=qt, oh=oh: e.dma_start(
                d_std[:, qt * 1024 + oh * 512: qt * 1024 + (oh + 1) * 512],
                dstage[:, oh * 512:(oh + 1) * 512]),
                waits=[(g("dev"), 2 * qt + oh + 1)], incs=[(g("sdst%d" % (qt % 2)), 16)])
        if qt % 2 == 1:
            emit_w1_half()
    while w1_half[0] < 8:
        emit_w1_half()

    w2_group = [0]
    def emit_w2_group():
        gg = w2_group[0]
        if gg >= 4:
            return
        for a in range(4):
            er = 8 + 2 * a
            k = gg * 4 + a
            for hf in range(2):
                waits = []
                if hf == 0:
                    waits = [(g("d2evw2"), k)] if k >= 1 else [(g("d2ev"), 8)]
                incs = [(g("d2er"), 1)] if hf == 1 else []
                Dp.op("tensor", lambda e, er=er, hf=hf, waits=waits: e.matmul(
                    PE2[:, hf * 512:(hf + 1) * 512],
                    selmat[:, er * 128:(er + 1) * 128], lora[:, hf * 512:(hf + 1) * 512],
                    start=True, stop=True), waits=waits, incs=incs)
            ew = [(g("d2er"), 9 + k), (g("d2erE"), 8 + max(k - 1, 0))]
            Dp.op("scalar", lambda e, gg=gg, k=k: e.activation(
                evr2[k % 2][:, :],
                PE2[:, 0:1024].rearrange("p (o e) -> p e o", e=4)[:, gg, :],
                AF.Identity), waits=ew, incs=[(g("d2evw2"), 1)])
            for c2 in range(8):
                ch = gg * 8 + c2
                incs = []
                if c2 == 7:
                    incs.append((g("d2erE"), 1))
                    incs.append((g("d2fw2"), 1))
                ws = [(g("sw2"), 96)] if gg < 3 else [(g("sw2b"), 32)]
                if c2 == 0:
                    ws.append((g("d2evw2"), k + 1))
                Dp.op("vector", lambda e, a=a, gg=gg, c2=c2, ch=ch, k=k: e.scalar_tensor_tensor(
                    w2t_sb[:, ch * 1024 + a * 256: ch * 1024 + (a + 1) * 256],
                    evr2[k % 2][:, :],
                    oddcT[:, c2 * 16 + 9 + 2 * a: c2 * 16 + 10 + 2 * a],
                    w2t_sb[:, ch * 1024 + a * 256: ch * 1024 + (a + 1) * 256],
                    OP.mult, OP.add),
                    waits=ws, incs=incs)
        w2_group[0] += 1

    def emit_ring_loads(tg):
        for qt2 in range(2):
            qt = 2 * tg + qt2
            waits = [(g("sdst%d" % (qt % 2)), 32 * (qt // 2 + 1))]
            if tg >= 1:
                waits.append((g("esd"), tg))
            Dp.op("sync", lambda e, qt=qt, qt2=qt2: e.dma_start(
                sgring[qt2][:, :], d_std[:, qt * 1024:(qt + 1) * 1024]),
                waits=waits, incs=[(g("ssgr%d" % qt2), 16)])
        if tg >= 1:
            waits = [(g("ep1"), 32 * (tg - 1))] if tg >= 2 else []
            Dp.op("sync", lambda e, tg=tg: e.dma_start(
                xnring[tg % 2][:, :].rearrange("p (c q) -> p c q", c=8),
                d_xnd[:, :].rearrange("p (c q) -> p c q", c=8)[:, :, tg * 256:(tg + 1) * 256]),
                waits=waits, incs=[(g("sxnr%d" % (tg % 2)), 16)])
        for qt2 in range(2):
            qt = 2 * tg + qt2
            waits = [(g("epf"), 2 * (qt - 1))] if qt >= 2 else [(g("dmm"), 32)]
            Dp.op("sync", lambda e, qt=qt: e.dma_start(
                xres[qt % 2][:, :], d_x[qt * 128:(qt + 1) * 128, :]),
                waits=waits, incs=[(g("sxre%d" % (qt % 2)), 16)])

    for tg in range(8):
        if tg == 0:
            emit_ring_loads(0)
        def emit_seeds(tg=tg):
            for qt2 in range(2):
                for oh in range(2):
                    b = qt2 * 2 + oh
                    waits = [(g("ssgr%d" % qt2), 16 * (tg + 1))]
                    if tg >= 1:
                        waits.append((g("epf"), 4 * (tg - 1) + b + 1))
                    else:
                        waits.append((g("dev"), 29 + b))
                    Dp.op("tensor", lambda e, b=b, qt2=qt2, oh=oh: e.matmul(
                        POUT[b][:, :], identB[:, :], sgring[qt2][:, oh * 512:(oh + 1) * 512],
                        start=True, stop=False), waits=waits)
                    waits2 = [(g("sbias"), 48)] if (tg == 0 and b == 0) else []
                    incs = [(g("esd"), 1)] if b == 3 else []
                    Dp.op("tensor", lambda e, b=b, oh=oh: e.matmul(
                        POUT[b][:, :], ones2[0:2, :], bb2[0:2, oh * 512:(oh + 1) * 512],
                        start=False, stop=False), waits=waits2, incs=incs)
        def emit_p2(m, tg=tg):
            for qt2 in range(2):
                for oh in range(2):
                    b = qt2 * 2 + oh
                    waits = []
                    if qt2 == 0 and oh == 0:
                        waits = [(g("egel"), 32 * tg + m + 1)]
                        if tg == 0:
                            waits.append((g("d2fw2"), 4 * (m // 8) + 4))
                    incs = [(g("ep2"), 1)] if b == 3 else []
                    Dp.op("tensor", lambda e, m=m, qt2=qt2, oh=oh, b=b: e.matmul(
                        POUT[b][:, :],
                        hgel[:, m * 256 + qt2 * 128: m * 256 + qt2 * 128 + 128],
                        w2t_sb[:, m * 1024 + oh * 512: m * 1024 + (oh + 1) * 512],
                        start=False, stop=(m == 31)), waits=waits, incs=incs)
        for m in range(32):
            for k in range(8):
                waits = []
                if k == 0:
                    waits = [(g("egel"), 32 * tg + m - 1)]
                    if m == 0:
                        waits.append((g("sxnr%d" % (tg % 2)), 16 * (tg // 2 + 1)))
                    if tg == 0:
                        waits.append((g("d2f"), m // 8 + 1))
                        if m == 0:
                            waits.append((g("sw1"), 128))
                incs = [(g("ep1"), 1)] if k == 7 else []
                Dp.op("tensor", lambda e, tg=tg, m=m, k=k: e.matmul(
                    PH[m % 2][:, :],
                    w1t_sb[:, k * 4096 + m * 128: k * 4096 + (m + 1) * 128],
                    xnring[tg % 2][:, k * 256:(k + 1) * 256],
                    start=(k == 0), stop=(k == 7)), waits=waits, incs=incs)
            if tg == 0 and m in (1, 9, 17, 25):
                emit_w2_group()
            if m == 2:
                emit_seeds()
            if m >= 3:
                emit_p2(m - 3)
            gw = [(g("ep1"), 32 * tg + m + 1), (g("ep2"), 32 * (tg - 1) + m + 1)]
            Dp.op("scalar", lambda e, tg=tg, m=m: e.activation(
                hgel[:, m * 256:(m + 1) * 256], PH[m % 2][:, :], AF.Gelu, bias=b1B[:, m:m + 1]),
                waits=gw, incs=[(g("egel"), 1)])
        if tg + 1 < 8:
            emit_ring_loads(tg + 1)
        for mm_ in (29, 30, 31):
            emit_p2(mm_)
        for qt2 in range(2):
            qt = 2 * tg + qt2
            for oh in range(2):
                b = qt2 * 2 + oh
                waits = [(g("ep2"), 32 * tg + 32), (g("sxre%d" % (qt % 2)), 16 * (qt // 2 + 1))]
                if oh == 0 and qt >= 2:
                    waits.append((g("sout%d" % (qt % 2)), 16 * (qt // 2)))
                Dp.op("vector", lambda e, qt=qt, oh=oh, b=b: e.tensor_tensor(
                    outsb[qt % 2][:, oh * 512:(oh + 1) * 512], POUT[b][:, :],
                    xres[qt % 2][:, oh * 512:(oh + 1) * 512], OP.add),
                    waits=waits, incs=[(g("epf"), 1)])
            Dp.op("sync", lambda e, qt=qt: e.dma_start(
                d_out[qt * 128:(qt + 1) * 128, :], outsb[qt % 2][:, :]),
                waits=[(g("epf"), 2 * (qt + 1))], incs=[(g("sout%d" % (qt % 2)), 16)])
    Dp.op("sync", lambda e: e.wait_ge(g("sout0").h, 16 * 8))
    Dp.op("sync", lambda e: e.wait_ge(g("sout1").h, 16 * 8))

    import contextlib
    with contextlib.ExitStack() as es:
        PA = [es.enter_context(nc.psum_tensor(f"pa{i}", [128, 512], BF16)).ap() for i in range(4)]
        PB = [es.enter_context(nc.psum_tensor(f"pb{i}", [128, 512], F32)).ap() for i in range(4)]
        A.emit(nc)
    with contextlib.ExitStack() as es:
        PSC = [es.enter_context(nc.psum_tensor(f"psc{i}", [128, 512], F32)).ap() for i in range(4)]
        PAV = [es.enter_context(nc.psum_tensor(f"pav{i}", [128, 512], F32)).ap() for i in range(4)]
        C.emit(nc)
    with contextlib.ExitStack() as es:
        PE2 = es.enter_context(nc.psum_tensor("pe2", [128, 1024], F32)).ap()
        PH = [es.enter_context(nc.psum_tensor(f"ph{i}", [128, 256], F32)).ap() for i in range(2)]
        POUT = [es.enter_context(nc.psum_tensor(f"pout{i}", [128, 512], F32)).ap() for i in range(4)]
        Dp.emit(nc)
    return nc


def make_consts():
    ki = np.arange(128, dtype=np.float64)
    cols = []
    for h in range(H):
        for kb in range(NKB):
            cols.append(SLOPES[h] * (kb * 128 + ki))
    alibi = np.stack(cols, axis=1).astype(np.float32)
    qpos = np.stack([qt * 128 + ki for qt in range(16)], axis=1).astype(np.float32)
    identB = np.eye(128, dtype=np.float32).astype(NPBF)
    kk, qq = np.meshgrid(np.arange(128), np.arange(128), indexing="ij")
    minmask = np.where(kk <= qq, 1e38, 0.0).astype(np.float32).astype(NPBF)
    amask = np.where(kk <= qq, 0.0, -240000.0).astype(np.float32)
    onescol = np.ones((1, 128), dtype=np.float32).astype(NPBF)
    vconst = np.stack([-(240.0 + 8.0 * SLOPES[h] * np.arange(S, dtype=np.float64))
                       for h in range(H)]).astype(np.float32).astype(NPBF)
    return alibi, qpos, identB, minmask, amask, onescol, vconst


_NC_CACHE = {}

def kernel(L0_attn_x, L0_attn_Wq, L0_attn_bq, L0_attn_Wk, L0_attn_bk, L0_attn_Wv,
           L0_attn_bv, L0_attn_Wo, L0_attn_bo, L0_ffn_W1, L0_ffn_b1, L0_ffn_W2,
           L0_ffn_b2, L0_ffn_scale):
    x = np.ascontiguousarray(np.asarray(L0_attn_x, dtype=np.float32))
    wq = np.ascontiguousarray(np.asarray(L0_attn_Wq, dtype=np.float32))
    wk = np.ascontiguousarray(np.asarray(L0_attn_Wk, dtype=np.float32))
    wv = np.ascontiguousarray(np.asarray(L0_attn_Wv, dtype=np.float32))
    wo = np.ascontiguousarray(np.asarray(L0_attn_Wo, dtype=np.float32))
    w1t = np.ascontiguousarray(np.asarray(L0_ffn_W1, dtype=np.float32).T)
    w2t = np.ascontiguousarray(np.asarray(L0_ffn_W2, dtype=np.float32).T)
    bqB = np.ascontiguousarray(np.asarray(L0_attn_bq, dtype=np.float32).reshape(8, 128).T)
    bkB = np.ascontiguousarray(np.asarray(L0_attn_bk, dtype=np.float32).reshape(8, 128).T)
    b1B = np.ascontiguousarray(np.asarray(L0_ffn_b1, dtype=np.float32).reshape(32, 128).T)
    bv = np.ascontiguousarray(np.asarray(L0_attn_bv, dtype=np.float32).reshape(1, D))
    bo = np.ascontiguousarray(np.asarray(L0_attn_bo, dtype=np.float32).reshape(1, D))
    b2 = np.ascontiguousarray(np.asarray(L0_ffn_b2, dtype=np.float32).reshape(1, D))
    scl = np.zeros((1, 8), dtype=np.float32)
    scl[0, 0] = np.asarray(L0_ffn_scale, dtype=np.float32)
    alibi, qpos, identB, minmask, amask, onescol, vconst = make_consts()

    nc = build_nc()
    selmat = np.zeros((16, 2048), dtype=np.float32)
    for r in range(16):
        selmat[r, r * 128:(r + 1) * 128] = 1.0
    selmat = selmat.astype(NPBF)
    common = dict(wq=wq, wk=wk, wv=wv, wo=wo, w1t=w1t, w2t=w2t, bqB=bqB, bkB=bkB,
                  b1B=b1B, bv=bv, bo=bo, b2=b2, scl=scl, alibi=alibi, qpos=qpos, vconst=vconst,
                  identb=identB, minmask=minmask, amask=amask, onescol=onescol, selmat=selmat)
    in_maps = [dict(common, x=np.ascontiguousarray(x[b])) for b in range(8)]
    res = run_bass_kernel_spmd(nc, in_maps, core_ids=list(range(8)))
    return np.stack([np.asarray(res.results[b]["out"]) for b in range(8)], axis=0)


